# version 03 (final)
# baseline (speedup 1.0000x reference)
import math
from collections import defaultdict
import contextlib
import numpy as np

import concourse.bass as bass
import concourse.mybir as mybir
from concourse.bass_utils import run_bass_kernel_spmd

F32 = mybir.dt.float32
BF16 = mybir.dt.bfloat16
I32 = mybir.dt.int32
AF = mybir.ActivationFunctionType
ALU = mybir.AluOpType
AX = mybir.AxisListType

NCORES = 8
D = 1024
TCORE = 4096
TT = 512
NTT = TCORE // TT
NC = D // 128



class Prog:

    ENG2BLOCK = {"pe": "tensor", "dve": "vector", "act": "scalar",
                 "pool": "gpsimd", "sp": "sync"}

    def __init__(self, nc):
        self.nc = nc
        self.items = []
        self.res = {}

    _POOL_SEMS = {"dva", "dvb", "dvc", "dvr", "dvw", "dchv", "dcS", "dcnw",
                  "dcwg", "dcwv", "dce2a", "dce2b", "dh1", "dh3", "dxr"}

    def op(self, eng, fn, reads=(), writes=(), dma=False, sem=None, tag=""):
        if dma and sem and (sem.startswith("dc") or sem in self._POOL_SEMS):
            pool = getattr(self, "_dcpool", None)
            if pool is None:
                pool = self._dcpool = {}
            sem = pool.setdefault(sem, f"dk{len(pool) % 12}")
        idx = len(self.items)
        deps = set()
        for r in reads:
            ent = self.res.get(r)
            if ent and ent[0] is not None:
                deps.add(ent[0])
        for w in writes:
            ent = self.res.get(w)
            if ent:
                if ent[0] is not None:
                    deps.add(ent[0])
                deps.update(ent[1])
        self.items.append(dict(eng=eng, fn=fn, dma=dma, sem=sem, idx=idx,
                               deps=deps, tag=tag, target=False))
        for r in reads:
            self.res.setdefault(r, [None, []])[1].append(idx)
        for w in writes:
            self.res[w] = [idx, []]
        return idx

    def dma(self, eng, fn, reads=(), writes=(), sem=None, tag=""):
        return self.op(eng, fn, reads, writes, dma=True, sem=sem, tag=tag)

    def barrier(self):
        idx = len(self.items)
        self.items.append(dict(eng="BARRIER", fn=None, dma=False, sem=None,
                               idx=idx, deps=set(), tag="barrier", target=False))
        self.res = {}
        self.epoch = getattr(self, "epoch", 0) + 1
        self._dcpool = {}
        return idx

    def finalize(self, es):
        nc = self.nc
        items = self.items
        for it in items:
            for d in it["deps"]:
                dit = items[d]
                if it["eng"] == "pe" and dit["eng"] == "pe" and not dit["dma"]:
                    continue
                dit["target"] = True
        last_per_eng = {}
        for it in items:
            if it["eng"] == "BARRIER":
                for e, j in last_per_eng.items():
                    if not items[j]["dma"]:
                        items[j]["target"] = True
            else:
                last_per_eng[it["eng"]] = it["idx"]
        for e, j in last_per_eng.items():
            if not items[j]["dma"]:
                items[j]["target"] = True
        sem_names = set()
        epoch = 0
        for it in items:
            if it["eng"] == "BARRIER":
                it["semn"] = None
                epoch += 1
                continue
            it["semn"] = ((it["sem"] or ("d_" + it["eng"])) if it["dma"]
                          else f"s_{it['eng']}_{epoch}")
            sem_names.add(it["semn"])
        counters = defaultdict(int)
        for it in items:
            if it["eng"] == "BARRIER":
                it["snapshot"] = dict(counters)
                it["val"], it["inc"] = None, 0
                continue
            if it["dma"]:
                counters[it["semn"]] += 16
                it["val"], it["inc"] = counters[it["semn"]], 16
            elif it["target"]:
                counters[it["semn"]] += 1
                it["val"], it["inc"] = counters[it["semn"]], 1
            else:
                it["val"], it["inc"] = None, 0
        self.final_counts = dict(counters)
        semh = {s: es.enter_context(nc.semaphore(s)) for s in sorted(sem_names)}
        self.semh = semh
        block = es.enter_context(nc.Block())
        present = [e for e in ("sp", "pe", "dve", "act", "pool")
                   if any(it["eng"] == e for it in items)]
        assert "sp" in present, "need sp engine for final waits"

        def make(eng_name):
            my_items = [it for it in items
                        if it["eng"] == eng_name or it["eng"] == "BARRIER"]

            def body(eng_obj):
                waited = defaultdict(int)
                for it in my_items:
                    if it["eng"] == "BARRIER":
                        for s in sorted(it["snapshot"]):
                            if waited[s] < it["snapshot"][s]:
                                eng_obj.wait_ge(semh[s], it["snapshot"][s])
                                waited[s] = it["snapshot"][s]
                        continue
                    wneed = {}
                    for d in it["deps"]:
                        dit = items[d]
                        if eng_name == "pe" and dit["eng"] == "pe" and not dit["dma"]:
                            continue
                        s, v = dit["semn"], dit["val"]
                        assert v is not None, (it["tag"], dit["tag"])
                        if v > wneed.get(s, 0):
                            wneed[s] = v
                    for s in sorted(wneed):
                        if waited[s] < wneed[s]:
                            eng_obj.wait_ge(semh[s], wneed[s])
                            waited[s] = wneed[s]
                    ins = it["fn"](eng_obj)
                    if it["inc"]:
                        ins.then_inc(semh[it["semn"]], it["inc"])
                if eng_name == "sp":
                    for s in sorted(self.final_counts):
                        if waited[s] < self.final_counts[s]:
                            eng_obj.wait_ge(semh[s], self.final_counts[s])
            return body

        for e in present:
            getattr(block, self.ENG2BLOCK[e])(make(e))


class Env:

    def __init__(self, nc, p, es):
        self.nc = nc
        self.p = p
        self.es = es
        self.cst = es.enter_context(nc.sbuf_tensor(_uname("cst"), [128, 32], F32))
        self._cst_vals = {}
        self._cst_next = 0
        self.ones1 = es.enter_context(nc.sbuf_tensor(_uname("ones1"), [1, 128], BF16))
        p.op("dve", lambda e: e.memset(self.ones1[:], 1.0), writes=[("ones1",)])
        self.ones1f = es.enter_context(nc.sbuf_tensor(_uname("ones1f"), [1, 128], F32))
        p.op("dve", lambda e: e.memset(self.ones1f[:], 1.0), writes=[("ones1f",)])
        self.ones_col = es.enter_context(nc.sbuf_tensor(_uname("ones_col"), [128, 1], BF16))
        p.op("dve", lambda e: e.memset(self.ones_col[:], 1.0), writes=[("ones_col",)])
        self.ones_colf = es.enter_context(nc.sbuf_tensor(_uname("ones_colf"), [128, 1], F32))
        p.op("dve", lambda e: e.memset(self.ones_colf[:], 1.0), writes=[("ones_colf",)])
        self.ps = [es.enter_context(nc.psum_tensor(_uname(f"ps{i}"), [128, 512], F32))
                   for i in range(6)]
        self.ps_stat = es.enter_context(nc.psum_tensor(_uname("ps_stat"), [128, 512], F32))
        self.ps_bf = es.enter_context(nc.psum_tensor(_uname("ps_bf"), [128, 1024], BF16))
        self._ps_rr = 0

    def const(self, val):
        val = float(val)
        if val not in self._cst_vals:
            k = self._cst_next
            assert k < 32
            self._cst_next += 1
            self._cst_vals[val] = k
            self.p.op("dve", lambda e, k=k, val=val: e.memset(self.cst[:, k:k + 1], val),
                      writes=[("cst", k)])
        k = self._cst_vals[val]
        return self.cst[:, k:k + 1]

    def const_key(self, val):
        return ("cst", self._cst_vals[float(val)])

    def psum(self):
        b = self.ps[self._ps_rr % 6]
        self._ps_rr += 1
        return b


def K(bank):
    return ("ps", id(bank))


_NAME_SEQ = [0]


def _uname(name):
    _NAME_SEQ[0] += 1
    return f"{name}_u{_NAME_SEQ[0]}"


def sbuf(nc, es, name, shape, dt=F32):
    return es.enter_context(nc.sbuf_tensor(_uname(name), list(shape), dt))



def load_x(p, env, x_hbm, xs, key="xs", T=TCORE):
    for c in range(NC):
        p.dma("sp", lambda e, c=c: e.dma_start(out=xs[:, c * T:(c + 1) * T],
                                               in_=x_hbm[c * 128:(c + 1) * 128, :]),
              writes=[(key, c, tt) for tt in range(T // TT)], sem=f"dx{c}")


def store_x(p, env, xs, y_hbm, key="xs", T=TCORE):
    for c in range(NC):
        p.dma("sp", lambda e, c=c: e.dma_start(out=y_hbm[c * 128:(c + 1) * 128, :],
                                               in_=xs[:, c * T:(c + 1) * T]),
              reads=[(key, c, tt) for tt in range(T // TT)], sem="dy")


def alloc_ln_tmps(nc, env, es):
    env.tmp_xb = sbuf(nc, es, "t_xb", (128, NC * TT), BF16)
    env.tmp_x2b = sbuf(nc, es, "t_x2b", (128, TT), BF16)
    env.tmp_rows = sbuf(nc, es, "t_rows", (1, 2 * TT), F32)
    env.tmp_mb = sbuf(nc, es, "t_mb", (128, TT), F32)
    env.tmp_rb = sbuf(nc, es, "t_rb", (128, TT), F32)


def ln_feat(p, env, srcs, skeys, *, cast=True, gamma=None, beta=None,
            gkeys=(), eps=1e-5, uid="ln", dst_aps=None, dst_keys=None):
    xb, x2b = env.tmp_xb, env.tmp_x2b
    rows = env.tmp_rows
    stat = env.ps_stat
    stat2 = env.psum()
    if cast:
        for c in range(NC):
            p.op("dve", lambda e, c=c: e.tensor_copy(xb[:, c * TT:(c + 1) * TT], srcs[c]),
                 reads=[skeys[c]], writes=[("xb", c)], tag=f"{uid}c{c}")
        work = [xb[:, c * TT:(c + 1) * TT] for c in range(NC)]
        wkeys = [("xb", c) for c in range(NC)]
    else:
        work = list(srcs)
        wkeys = list(skeys)
    for c in range(NC):
        p.op("pe", lambda e, c=c: e.matmul(stat[0:1, 0:TT], env.ones_col[:], work[c],
                                           start=(c == 0), stop=(c == NC - 1)),
             reads=[wkeys[c], ("ones_col",)], writes=[K(stat)])
    for c in range(NC):
        p.op("dve", lambda e, c=c: e.tensor_mul(x2b[:], work[c], work[c]),
             reads=[wkeys[c]], writes=[("x2b",)])
        p.op("pe", lambda e, c=c, stat2=stat2: e.matmul(stat2[0:1, 0:TT], env.ones_col[:], x2b[:],
                                                        start=(c == 0), stop=(c == NC - 1)),
             reads=[("x2b",), ("ones_col",)], writes=[K(stat2)])
    rA = rows[0:1, 0:TT]
    rB = rows[0:1, TT:2 * TT]
    p.op("dve", lambda e: e.tensor_scalar_mul(rA, stat[0:1, 0:TT], 1.0 / D),
         reads=[K(stat)], writes=[("rA",)])
    p.op("dve", lambda e: e.tensor_mul(rB, rA, rA),
         reads=[("rA",)], writes=[("rB",)])
    p.op("dve", lambda e, stat2=stat2: e.scalar_tensor_tensor(rB, stat2[0:1, 0:TT], 1.0 / D, rB,
                                                              ALU.mult, ALU.subtract),
         reads=[K(stat2), ("rB",)], writes=[("rB",)])
    epsap = env.const(eps)
    p.op("act", lambda e: e.activation(out=rB, in_=rB, func=AF.Sqrt,
                                       bias=epsap[0:1, :]),
         reads=[("rB",), env.const_key(eps)], writes=[("rB",)])
    p.op("dve", lambda e: e.tensor_scalar_mul(rA, rA, -1.0),
         reads=[("rA",)], writes=[("rA",)])
    mb, rb = env.tmp_mb, env.tmp_rb
    bc = env.psum()
    p.op("pe", lambda e, bc=bc: e.matmul(bc[:, 0:TT], env.ones1f[:], rA,
                                         start=True, stop=True),
         reads=[("rA",), ("ones1f",)], writes=[K(bc)])
    p.op("dve", lambda e, bc=bc: e.tensor_copy(mb[:], bc[:, 0:TT]),
         reads=[K(bc)], writes=[("mb",)])
    p.op("dve", lambda e: e.reciprocal(rA, rB),
         reads=[("rB",)], writes=[("rA",)])
    bc2 = env.psum()
    p.op("pe", lambda e, bc2=bc2: e.matmul(bc2[:, 0:TT], env.ones1f[:], rA,
                                           start=True, stop=True),
         reads=[("rA",), ("ones1f",)], writes=[K(bc2)])
    p.op("dve", lambda e, bc2=bc2: e.tensor_copy(rb[:], bc2[:, 0:TT]),
         reads=[K(bc2)], writes=[("rb",)])
    if dst_aps is None:
        dst_aps, dst_keys = work, wkeys
    for c in range(NC):
        p.op("dve", lambda e, c=c: e.tensor_add(work[c], work[c], mb[:]),
             reads=[wkeys[c], ("mb",)], writes=[wkeys[c]])
        if gamma is None:
            p.op("dve", lambda e, c=c: e.tensor_mul(dst_aps[c], work[c], rb[:]),
                 reads=[wkeys[c], ("rb",)], writes=[dst_keys[c]])
        else:
            p.op("dve", lambda e, c=c: e.tensor_mul(work[c], work[c], rb[:]),
                 reads=[wkeys[c], ("rb",)], writes=[wkeys[c]])
            p.op("act", lambda e, c=c: e.activation(out=dst_aps[c], in_=work[c],
                                                    func=AF.Identity, scale=gamma[c], bias=beta[c]),
                 reads=[wkeys[c]] + list(gkeys), writes=[dst_keys[c]])
    return dst_aps, dst_keys


def ln_top(p, env, xs, tt, *, T=TCORE, xkey="xs", gamma=None, beta=None,
           gb_keys=(), eps=1e-5, uid="ln"):
    srcs = [xs[:, c * T + tt * TT: c * T + tt * TT + TT] for c in range(NC)]
    skeys = [(xkey, c, tt) for c in range(NC)]
    return ln_feat(p, env, srcs, skeys, cast=True, gamma=gamma, beta=beta,
                   gkeys=gb_keys, eps=eps, uid=uid)


def stream_gemm(p, env, *, wpre_h, nk, nm, rhs_ap, rhs_keys, out_fn, slots,
                slot_base, sem, mj=128, tag=""):
    def issue_dma(m):
        slot = slots[m % 2]
        p.dma("pool", lambda e, m=m, slot=slot: e.dma_start(
            out=slot[:, 0:nk * mj].rearrange("p (k j) -> p k j", j=mj), in_=wpre_h[m]),
            writes=[(slot_base, m % 2)], sem=f"{sem}{m % 2}", tag=f"{tag}w{m}")

    issue_dma(0)
    if nm > 1:
        issue_dma(1)
    for m in range(nm):
        slot = slots[m % 2]
        bank = env.psum()
        for k in range(nk):
            n = rhs_ap(k).shape[1]
            p.op("pe", lambda e, m=m, k=k, n=n, slot=slot, bank=bank: e.matmul(
                bank[:, 0:n], slot[:, k * mj:(k + 1) * mj], rhs_ap(k),
                start=(k == 0), stop=(k == nk - 1)),
                reads=[(slot_base, m % 2)] + list(rhs_keys(k)), writes=[K(bank)],
                tag=f"{tag}mm{m},{k}")
        out_fn(m, bank)
        if m + 2 < nm:
            issue_dma(m + 2)



def build_l1ffn(shared=None, x_ap=None, y_ap=None, pfx=""):
    if shared is None:
        nc = bass.Bass("TRN2", target_bir_lowering=False)
        p = Prog(nc)
    else:
        nc, p = shared

    def din(name, shape):
        return nc.dram_tensor(pfx + name, list(shape), F32, kind="ExternalInput").ap()

    x_h = x_ap if x_ap is not None else din("x", [D, TCORE])
    w1_h = din("w1", [32, 128, 8, 128])
    b1_h = din("b1", [128, 32])
    w2_h = din("w2", [8, 128, 32, 128])
    b2_h = din("b2", [128, 8])
    y_h = y_ap if y_ap is not None else nc.dram_tensor(
        pfx + "y", [D, TCORE], F32, kind="ExternalOutput").ap()

    with contextlib.ExitStack() as es:
        env = Env(nc, p, es)
        env.const(1e-5)
        alloc_ln_tmps(nc, env, es)
        xt = sbuf(nc, es, "xt", (128, NC * TT), F32)
        hb = sbuf(nc, es, "hb", (128, 32 * TT), BF16)
        b1s = sbuf(nc, es, "b1s", (128, 32), F32)
        b2s = sbuf(nc, es, "b2s", (128, 8), F32)
        w1b = [sbuf(nc, es, f"w1b{i}", (128, 8 * 128), BF16) for i in range(2)]
        w2b = [sbuf(nc, es, f"w2b{i}", (128, 32 * 128), BF16) for i in range(2)]

        p.dma("sp", lambda e: e.dma_start(out=b1s[:], in_=b1_h[:, :]), writes=[("b1s",)], sem="dva")
        p.dma("sp", lambda e: e.dma_start(out=b2s[:], in_=b2_h[:, :]), writes=[("b2s",)], sem="dvb")

        for tt in range(NTT):
            for c in range(NC):
                p.dma("sp", lambda e, c=c, tt=tt: e.dma_start(
                    out=xt[:, c * TT:(c + 1) * TT],
                    in_=x_h[c * 128:(c + 1) * 128, tt * TT:(tt + 1) * TT]),
                    writes=[("xt", c)], sem=f"dx{c}")
            ln_aps, ln_keys = ln_feat(
                p, env, [xt[:, c * TT:(c + 1) * TT] for c in range(NC)],
                [("xt", c) for c in range(NC)], cast=True, uid=f"ln{tt}")

            def epi1(m, bank):
                p.op("act", lambda e, m=m, bank=bank: e.activation(
                    out=hb[:, m * TT:(m + 1) * TT], in_=bank[:, 0:TT], func=AF.Gelu,
                    bias=b1s[:, m:m + 1]),
                    reads=[K(bank), ("b1s",)], writes=[("hb", m)])

            stream_gemm(p, env, wpre_h=w1_h, nk=8, nm=32,
                        rhs_ap=lambda k: ln_aps[k],
                        rhs_keys=lambda k: [ln_keys[k]],
                        out_fn=epi1, slots=w1b, slot_base="w1b", sem="dw1",
                        tag=f"g1t{tt}")

            def epi2(m2, bank):
                p.op("dve", lambda e, m2=m2, bank=bank: e.scalar_tensor_tensor(
                    xt[:, m2 * TT:(m2 + 1) * TT], bank[:, 0:TT], b2s[:, m2:m2 + 1],
                    xt[:, m2 * TT:(m2 + 1) * TT], ALU.add, ALU.add),
                    reads=[K(bank), ("b2s",), ("xt", m2)], writes=[("xt", m2)])
                p.dma("sp", lambda e, m2=m2, tt=tt, bank=bank: e.dma_start(
                    out=y_h[m2 * 128:(m2 + 1) * 128, tt * TT:(tt + 1) * TT],
                    in_=xt[:, m2 * TT:(m2 + 1) * TT]),
                    reads=[("xt", m2)], sem="dy")

            stream_gemm(p, env, wpre_h=w2_h, nk=32, nm=8,
                        rhs_ap=lambda k: hb[:, k * TT:(k + 1) * TT],
                        rhs_keys=lambda k: [("hb", k)],
                        out_fn=epi2, slots=w2b, slot_base="w2b", sem="dw2",
                        tag=f"g2t{tt}")
        if shared is None:
            p.finalize(es)
    return nc



def make_ident(p, env, es):
    nc = env.nc
    env.identb = sbuf(nc, es, "identb", (128, 128), BF16)
    env.id_scr = sbuf(nc, es, "id_scr", (128, 128), F32)
    p.op("pool", lambda e: e.iota(env.id_scr[:], [[1, 128]], channel_multiplier=-1,
                                  allow_small_or_imprecise_dtypes=True),
         writes=[("id_scr",)])
    p.op("dve", lambda e: e.tensor_single_scalar(env.identb[:], env.id_scr[:], 0.0,
                                                 ALU.is_equal),
         reads=[("id_scr",)], writes=[("identb",)])


def build_l0attnb(shared=None, x_ap=None, y_ap=None, pfx=""):
    if shared is None:
        nc = bass.Bass("TRN2", target_bir_lowering=False)
        p = Prog(nc)
    else:
        nc, p = shared

    def din(name, shape):
        return nc.dram_tensor(pfx + name, list(shape), F32, kind="ExternalInput").ap()

    x_h = x_ap if x_ap is not None else din("x", [D, TCORE])
    wq_h = din("wq", [8, 128, 8, 128])
    wk_h = din("wk", [8, 128, 8, 128])
    wv_h = din("wv", [128, 8, 1024])
    bq_h = din("bq", [128, 8])
    bk_h = din("bk", [128, 8])
    bv_h = din("bv", [1, 1024])
    gm_h = din("gm", [128, 8])
    bt_h = din("bt", [128, 8])
    y_h = y_ap if y_ap is not None else nc.dram_tensor(
        pfx + "y", [D, TCORE], F32, kind="ExternalOutput").ap()

    with contextlib.ExitStack() as es:
        env = Env(nc, p, es)
        env.const(1e-5)
        alloc_ln_tmps(nc, env, es)
        make_ident(p, env, es)
        xs = sbuf(nc, es, "xs", (128, NC * TT), F32)
        qf = sbuf(nc, es, "qf", (128, NC * TT), BF16)
        kf = sbuf(nc, es, "kf", (128, NC * TT), BF16)
        vt = sbuf(nc, es, "vt", (128, 4 * 1024), BF16)
        wvs = sbuf(nc, es, "wvs", (128, 8 * 1024), BF16)
        rsb = sbuf(nc, es, "rsb", (128, NC * TT), BF16)
        bqs = sbuf(nc, es, "bqs", (128, 8), F32)
        bks = sbuf(nc, es, "bks", (128, 8), F32)
        gms = sbuf(nc, es, "gms", (128, 8), F32)
        bts = sbuf(nc, es, "bts", (128, 8), F32)
        bvr = sbuf(nc, es, "bvr", (1, 1024), BF16)
        bvb = sbuf(nc, es, "bvb", (128, 1024), BF16)
        wslot = [sbuf(nc, es, f"wsl{i}", (128, 8 * 128), BF16) for i in range(2)]
        tw = sbuf(nc, es, "tw", (128, 3 * 128), F32)
        etile = sbuf(nc, es, "etile", (128, 128), BF16)
        mtile = sbuf(nc, es, "mtile", (128, 128), BF16)
        m8b = sbuf(nc, es, "m8b", (128, 36), F32)
        pbt = sbuf(nc, es, "pbt", (128, 128), BF16)
        ptb = sbuf(nc, es, "ptb", (128, 4 * 128), BF16)
        ew = sbuf(nc, es, "ew", (128, 512), BF16)

        for ii, (t, hh) in enumerate((("bqs", bq_h), ("bks", bk_h), ("gms", gm_h), ("bts", bt_h))):
            tt_ = {"bqs": bqs, "bks": bks, "gms": gms, "bts": bts}[t]
            p.dma("sp", lambda e, tt_=tt_, hh=hh: e.dma_start(out=tt_[:], in_=hh[:, :]),
                  writes=[(t,)], sem=f"dv{ii}")
        p.dma("pool", lambda e: e.dma_start(out=bvr[:], in_=bv_h[:, :]),
              writes=[("bvr",)], sem="dvr")
        p.dma("pool", lambda e: e.dma_start(out=wvs[:].rearrange("p (c j) -> p c j", j=1024),
                                            in_=wv_h[:, :, :]),
              writes=[("wvs",)], sem="dvw")
        for n in range(2):
            bb = env.psum()
            p.op("pe", lambda e, n=n, bb=bb: e.matmul(bb[:, 0:512], env.ones1[:],
                                                      bvr[0:1, n * 512:(n + 1) * 512],
                                                      start=True, stop=True),
                 reads=[("bvr",), ("ones1",)], writes=[K(bb)])
            p.op("dve", lambda e, n=n, bb=bb: e.tensor_copy(bvb[:, n * 512:(n + 1) * 512],
                                                            bb[:, 0:512]),
                 reads=[K(bb)], writes=[("bvb",)])

        for tt in range(NTT):
            for c in range(NC):
                p.dma("sp", lambda e, c=c, tt=tt: e.dma_start(
                    out=xs[:, c * TT:(c + 1) * TT],
                    in_=x_h[c * 128:(c + 1) * 128, tt * TT:(tt + 1) * TT]),
                    writes=[("xs", c, tt)], sem=f"dx{c}")
            ln_aps, ln_keys = ln_feat(
                p, env, [xs[:, c * TT:(c + 1) * TT] for c in range(NC)],
                [("xs", c, tt) for c in range(NC)], cast=True, uid=f"ln{tt}")

            def epi_q(m, bank, dst=qf, dkey="qf", bias=bqs, bkey=("bqs",)):
                p.op("act", lambda e, m=m, bank=bank, dst=dst, bias=bias: e.activation(
                    out=dst[:, m * TT:(m + 1) * TT], in_=bank[:, 0:TT], func=AF.Identity,
                    bias=bias[:, m:m + 1]),
                    reads=[K(bank), bkey], writes=[(dkey, m)])

            stream_gemm(p, env, wpre_h=wq_h, nk=8, nm=8,
                        rhs_ap=lambda k: ln_aps[k], rhs_keys=lambda k: [ln_keys[k]],
                        out_fn=lambda m, bank: epi_q(m, bank, qf, "qf", bqs, ("bqs",)),
                        slots=wslot, slot_base="wsl", sem="dwq", tag=f"q{tt}")
            stream_gemm(p, env, wpre_h=wk_h, nk=8, nm=8,
                        rhs_ap=lambda k: ln_aps[k], rhs_keys=lambda k: [ln_keys[k]],
                        out_fn=lambda m, bank: epi_q(m, bank, kf, "kf", bks, ("bks",)),
                        slots=wslot, slot_base="wsl", sem="dwq", tag=f"k{tt}")
            for t4 in range(4):
                for n in range(2):
                    bank = env.psum()
                    for c in range(NC):
                        p.op("pe", lambda e, t4=t4, n=n, c=c, bank=bank: e.matmul(
                            bank[:, 0:512],
                            ln_aps[c][:, t4 * 128:(t4 + 1) * 128],
                            wvs[:, c * 1024 + n * 512: c * 1024 + (n + 1) * 512],
                            start=(c == 0), stop=(c == NC - 1)),
                            reads=[ln_keys[c], ("wvs",)], writes=[K(bank)])
                    p.op("dve", lambda e, t4=t4, n=n, bank=bank: e.tensor_add(
                        vt[:, t4 * 1024 + n * 512: t4 * 1024 + (n + 1) * 512],
                        bank[:, 0:512], bvb[:, n * 512:(n + 1) * 512]),
                        reads=[K(bank), ("bvb",)], writes=[("vt", t4)])
            for w in range(4):
                sbank = env.psum()
                for c in range(NC):
                    p.op("pe", lambda e, w=w, c=c, sbank=sbank: e.matmul(
                        sbank[:, 0:128],
                        qf[:, c * TT + w * 128: c * TT + (w + 1) * 128],
                        kf[:, c * TT + w * 128: c * TT + (w + 1) * 128],
                        start=(c == 0), stop=(c == NC - 1)),
                        reads=[("qf", c), ("kf", c)], writes=[K(sbank)])
                p.op("dve", lambda e, sbank=sbank: e.tensor_scalar_mul(
                    tw[:, 0:128], sbank[:, 0:128], 1.0 / 32.0),
                    reads=[K(sbank)], writes=[("tw0",)])
                cur, curkey = 0, ("tw0",)
                for r in range(4):
                    p.op("dve", lambda e, r=r, cur=cur: e.max(m8b[:, r * 8:(r + 1) * 8],
                                                              tw[:, cur * 128:(cur + 1) * 128]),
                         reads=[curkey], writes=[("m8", r)])
                    if r < 3:
                        nxt = 1 + (r % 2)
                        p.op("dve", lambda e, r=r, cur=cur, nxt=nxt: e.match_replace(
                            tw[:, nxt * 128:(nxt + 1) * 128], m8b[:, r * 8:(r + 1) * 8],
                            tw[:, cur * 128:(cur + 1) * 128], -1e30),
                             reads=[curkey, ("m8", r)], writes=[("tw", nxt)])
                        cur, curkey = nxt, ("tw", nxt)
                p.op("dve", lambda e: e.tensor_scalar_mul(m8b[:, 32:33], m8b[:, 0:1], -1.0),
                     reads=[("m8", 0)], writes=[("nmx",)])
                p.op("act", lambda e: e.activation(out=etile[:], in_=tw[:, 0:128],
                                                   func=AF.Exp, bias=m8b[:, 32:33]),
                     reads=[("tw0",), ("nmx",)], writes=[("etile",)])
                p.op("dve", lambda e: e.tensor_scalar(mtile[:], tw[:, 0:128],
                                                      m8b[:, 31:32], None, ALU.is_ge),
                     reads=[("tw0",), ("m8", 3)], writes=[("mtile",)])
                p.op("dve", lambda e: e.tensor_mul(etile[:], etile[:], mtile[:]),
                     reads=[("etile",), ("mtile",)], writes=[("etile",)])
                p.op("dve", lambda e: e.tensor_reduce(m8b[:, 33:34], etile[:], AX.X, ALU.add),
                     reads=[("etile",)], writes=[("dcol",)])
                p.op("dve", lambda e: e.reciprocal(m8b[:, 34:35], m8b[:, 33:34]),
                     reads=[("dcol",)], writes=[("rdcol",)])
                p.op("dve", lambda e: e.tensor_scalar(pbt[:], etile[:], m8b[:, 34:35],
                                                      None, ALU.mult),
                     reads=[("etile",), ("rdcol",)], writes=[("pbt",)])
                p.op("pe", lambda e: e.transpose(env.ps_bf[:, 0:128], pbt[:], env.identb[:]),
                     reads=[("pbt",), ("identb",)], writes=[K(env.ps_bf)])
                p.op("dve", lambda e, w=w: e.tensor_copy(ptb[:, w * 128:(w + 1) * 128],
                                                         env.ps_bf[:, 0:128]),
                     reads=[K(env.ps_bf)], writes=[("ptb", w)])
            for vf in range(8):
                bank = env.psum()
                for w in range(4):
                    p.op("pe", lambda e, vf=vf, w=w, bank=bank: e.matmul(
                        bank[:, w * 128:(w + 1) * 128],
                        vt[:, w * 1024 + vf * 128: w * 1024 + (vf + 1) * 128],
                        ptb[:, w * 128:(w + 1) * 128],
                        start=True, stop=True),
                        reads=[("vt", w), ("ptb", w)], writes=[K(bank)])
                p.op("act", lambda e, bank=bank: e.activation(out=ew[:, 0:512],
                                                              in_=bank[:, 0:512], func=AF.Abs),
                     reads=[K(bank)], writes=[("ew0",)])
                p.op("dve", lambda e, bank=bank: e.scalar_tensor_tensor(
                    ew[:, 0:512], ew[:, 0:512], 0.495 / 0.505, bank[:, 0:512],
                    ALU.mult, ALU.add),
                    reads=[("ew0",), K(bank)], writes=[("ew0",)])
                p.op("dve", lambda e, vf=vf: e.scalar_tensor_tensor(
                    rsb[:, vf * TT:(vf + 1) * TT], ew[:, 0:512], 0.505, ln_aps[vf],
                    ALU.mult, ALU.add),
                    reads=[("ew0",), ln_keys[vf]], writes=[("rsb", vf)])
            r_aps, r_keys = ln_feat(p, env,
                                    [rsb[:, c * TT:(c + 1) * TT] for c in range(NC)],
                                    [("rsb", c) for c in range(NC)], cast=False,
                                    gamma=[gms[:, c:c + 1] for c in range(NC)],
                                    beta=[bts[:, c:c + 1] for c in range(NC)],
                                    gkeys=[("gms",), ("bts",)], uid=f"ln2_{tt}")
            for c in range(NC):
                p.op("dve", lambda e, c=c, tt=tt: e.tensor_add(
                    xs[:, c * TT:(c + 1) * TT],
                    xs[:, c * TT:(c + 1) * TT], r_aps[c]),
                    reads=[("xs", c, tt), r_keys[c]], writes=[("xs", c, tt)])
                p.dma("sp", lambda e, c=c, tt=tt: e.dma_start(
                    out=y_h[c * 128:(c + 1) * 128, tt * TT:(tt + 1) * TT],
                    in_=xs[:, c * TT:(c + 1) * TT]),
                    reads=[("xs", c, tt)], sem="dy")

        if shared is None:
            p.finalize(es)
    return nc


def run_l0attnb(x_full, Wq, bq, Wk, bk, Wv, bv, gamma, beta):
    nc = get_prog("l0attnb", build_l0attnb)
    xp = shard_x(x_full)
    wq = w_pre(np.ascontiguousarray(Wq.T))
    wk = w_pre(np.ascontiguousarray(Wk.T))
    wv = np.ascontiguousarray(Wv.T.reshape(8, 128, 1024).transpose(1, 0, 2))
    in_maps = [{"x": xp[c], "wq": wq, "wk": wk, "wv": wv,
                "bq": vec_pc(bq), "bk": vec_pc(bk), "bv": bv.reshape(1, 1024),
                "gm": vec_pc(gamma), "bt": vec_pc(beta)} for c in range(NCORES)]
    res = run_bass_kernel_spmd(nc, in_maps, core_ids=list(range(NCORES)))
    return unshard_x([np.asarray(res.results[c]["y"]) for c in range(NCORES)])



DBG_TAPS = {}
DBG_ON = False


def dbg_tap(p, env, nc, name, ap, keys, cols):
    if not DBG_ON:
        return
    if name not in DBG_TAPS:
        DBG_TAPS[name] = nc.dram_tensor("dbg_" + name, [128, cols], F32,
                                        kind="ExternalOutput").ap()
    h = DBG_TAPS[name]
    env._dbg_off = getattr(env, "_dbg_off", {})
    off = env._dbg_off.get(name, 0)
    env._dbg_off[name] = off + ap.shape[1]
    scr = env.dbg_scr
    p.op("dve", lambda e: e.tensor_copy(scr[:, 0:ap.shape[1]], ap),
         reads=list(keys), writes=[("dbgscr",)])
    p.dma("sp", lambda e: e.dma_start(out=h[:, off:off + ap.shape[1]],
                                      in_=scr[:, 0:ap.shape[1]]),
          reads=[("dbgscr",)], sem="ddbg")



def build_l1attnb(shared=None, x_ap=None, y_ap=None, pfx=""):
    if shared is None:
        nc = bass.Bass("TRN2", target_bir_lowering=False)
        p0 = Prog(nc)
    else:
        nc, p0 = shared

    def din(name, shape):
        return nc.dram_tensor(pfx + name, list(shape), F32, kind="ExternalInput").ap()

    x_h = x_ap if x_ap is not None else din("x", [D, TCORE])
    vecs_h = din("vecs", [128, 104])
    ci_h = din("ci", [128, 24])
    hhal_h = din("hhal", [128, 16])
    wk_h = din("wk", [8, 128, 8, 128])
    wv_h = din("wv", [8, 128, 8, 128])
    wr_h = din("wr", [8, 128, 8, 128])
    wo_h = din("wo", [8, 128, 8, 128])
    wfk_h = din("wfk", [32, 128, 8, 128])
    wfv_h = din("wfv", [8, 128, 32, 128])
    wfr_h = din("wfr", [8, 128, 8, 128])
    y_h = y_ap if y_ap is not None else nc.dram_tensor(
        pfx + "y", [D, TCORE], F32, kind="ExternalOutput").ap()
    st_h = nc.dram_tensor(pfx + "st", [128, 40], F32, kind="ExternalOutput").ap()

    VN = {n: i for i, n in enumerate(
        ["ln0w", "ln0b", "ln1w", "ln1b", "ln2w", "ln2b",
         "td", "tf", "tmk", "tmv", "tmr", "cmk", "cmr"])}

    with contextlib.ExitStack() as es:
        p = p0
        env = Env(nc, p, es)
        env.const(1e-5)
        alloc_ln_tmps(nc, env, es)
        xt = sbuf(nc, es, "xt", (128, NC * TT), F32)
        x2b = sbuf(nc, es, "x2b", (128, NC * TT), BF16)
        ht = sbuf(nc, es, "ht", (128, NC * 513), BF16)
        ar = [sbuf(nc, es, f"ar{i}", (128, 4096), BF16) for i in range(4)]
        srb = sbuf(nc, es, "srb", (128, NC * TT), BF16)
        vecs = sbuf(nc, es, "vcs", (128, 104), F32)
        cary = sbuf(nc, es, "cary", (128, 32), F32)
        colsave = sbuf(nc, es, "colsave", (128, 16), BF16)
        stt_ = sbuf(nc, es, "stt", (128, 40), F32)
        ewc = sbuf(nc, es, "ewc", (128, 8), F32)
        kfw = sbuf(nc, es, "kf32", (128, TT), F32)
        vfw = sbuf(nc, es, "vf32", (128, TT), F32)
        s0 = sbuf(nc, es, "s0", (128, TT), F32)
        s1 = sbuf(nc, es, "s1", (128, TT), F32)
        s2 = sbuf(nc, es, "s2", (128, TT), F32)
        ewb = sbuf(nc, es, "ewb", (128, TT), F32)
        tsub = sbuf(nc, es, "tsub", (128, TT), BF16)
        mcol = sbuf(nc, es, "mcol", (128, 4), F32)
        numt = sbuf(nc, es, "numt", (128, 513), F32)
        dent = sbuf(nc, es, "dent", (128, 513), F32)
        wsl = [sbuf(nc, es, f"wsl{i}", (128, 8 * 128), BF16) for i in range(2)]
        wbig = [sbuf(nc, es, f"wbig{i}", (128, 32 * 128), BF16) for i in range(2)]

        def vcol(name):
            return vecs[:, VN[name] * 8:(VN[name] + 1) * 8]

        def vc(name, c):
            i = VN[name] * 8 + c
            return vecs[:, i:i + 1]

        p.dma("sp", lambda e: e.dma_start(out=vecs[:], in_=vecs_h[:, :]),
              writes=[("vecs",)], sem="dva")
        p.dma("sp", lambda e: e.dma_start(out=cary[:, 0:24], in_=ci_h[:, :]),
              writes=[("cary", c) for c in range(NC)], sem="dvb")
        p.dma("pool", lambda e: e.dma_start(out=colsave[:], in_=hhal_h[:, :]),
              writes=[("colsave", 0), ("colsave", 1)], sem="dvc")
        p.op("act", lambda e: e.activation(out=ewc[:], in_=vcol("td"), func=AF.Exp),
             reads=[("vecs",)], writes=[("ewc",)])
        p.op("dve", lambda e: e.tensor_scalar_mul(ewc[:], ewc[:], -1.0),
             reads=[("ewc",)], writes=[("ewc",)])
        p.op("act", lambda e: e.activation(out=ewc[:], in_=ewc[:], func=AF.Exp),
             reads=[("ewc",)], writes=[("ewc",)])

        ones512 = sbuf(nc, es, "ones512", (128, TT), F32)
        p.op("dve", lambda e: e.memset(ones512[:], 1.0), writes=[("ones512",)])
        env.dbg_scr = sbuf(nc, es, "dbgscr", (128, TT), F32)
        DBG_TAPS.clear()

        for tt in range(NTT):
            for c in range(NC):
                p.dma("sp", lambda e, c=c, tt=tt: e.dma_start(
                    out=xt[:, c * TT:(c + 1) * TT],
                    in_=x_h[c * 128:(c + 1) * 128, tt * TT:(tt + 1) * TT]),
                    writes=[("xt", c)], sem=f"dx{c}")
            srcs = [xt[:, c * TT:(c + 1) * TT] for c in range(NC)]
            skeys = [("xt", c) for c in range(NC)]
            ln_feat(p, env, srcs, skeys, cast=True, uid=f"lnt{tt}")
            if tt < 2:
                dbg_tap(p, env, nc, "lnx", env.tmp_xb[:, 0:TT], [("xb", 0)], 1024)
            w_aps = [env.tmp_xb[:, c * TT:(c + 1) * TT] for c in range(NC)]
            w_keys = [("xb", c) for c in range(NC)]
            ln_feat(p, env, w_aps, w_keys, cast=False,
                    gamma=[vc("ln0w", c) for c in range(NC)],
                    beta=[vc("ln0b", c) for c in range(NC)], gkeys=[("vecs",)],
                    dst_aps=[x2b[:, c * TT:(c + 1) * TT] for c in range(NC)],
                    dst_keys=[("x2b", c) for c in range(NC)], uid=f"ln0_{tt}")
            if tt < 2:
                dbg_tap(p, env, nc, "xp", x2b[:, 0:TT], [("x2b", 0)], 1024)
            for c in range(NC):
                p.op("dve", lambda e, c=c: e.tensor_copy(ht[:, c * 513:c * 513 + 1],
                                                         colsave[:, c:c + 1]),
                     reads=[("colsave", 0)], writes=[("ht", c)])
            ln_feat(p, env, [x2b[:, c * TT:(c + 1) * TT] for c in range(NC)],
                    [("x2b", c) for c in range(NC)], cast=True,
                    gamma=[vc("ln1w", c) for c in range(NC)],
                    beta=[vc("ln1b", c) for c in range(NC)], gkeys=[("vecs",)],
                    dst_aps=[ht[:, c * 513 + 1:c * 513 + 513] for c in range(NC)],
                    dst_keys=[("ht", c) for c in range(NC)], uid=f"ln1_{tt}")
            for c in range(NC):
                p.op("dve", lambda e, c=c: e.tensor_copy(colsave[:, c:c + 1],
                                                         ht[:, c * 513 + 512:c * 513 + 513]),
                     reads=[("ht", c)], writes=[("colsave", 0)])
            if tt < 2:
                dbg_tap(p, env, nc, "h", ht[:, 1:513], [("ht", 0)], 1024)
            for c in range(NC):
                hcur = ht[:, c * 513 + 1:c * 513 + 513]
                xx = ht[:, c * 513:c * 513 + 512]
                p.op("dve", lambda e, c=c, hcur=hcur, xx=xx: e.tensor_sub(tsub[:], hcur, xx),
                     reads=[("ht", c)], writes=[("tsub",)])
                for ai, nm in ((0, "tmk"), (1, "tmv"), (2, "tmr")):
                    p.op("dve", lambda e, c=c, ai=ai, nm=nm, xx=xx: e.scalar_tensor_tensor(
                        ar[ai][:, c * TT:(c + 1) * TT], tsub[:], vc(nm, c), xx,
                        ALU.mult, ALU.add),
                        reads=[("tsub",), ("ht", c), ("vecs",)], writes=[("ar", ai, c)])
            def wslab(wh, m, slot_i, semn):
                p.dma("pool", lambda e, wh=wh, m=m, slot_i=slot_i: e.dma_start(
                    out=wsl[slot_i][:].rearrange("p (k j) -> p k j", j=128), in_=wh[m]),
                    writes=[("wsl", slot_i)], sem=f"{semn}{slot_i}")

            for c in range(NC):
                wslab(wr_h, c, c % 2, "dwr")
                bank = env.psum()
                for j in range(NC):
                    p.op("pe", lambda e, c=c, j=j, bank=bank: e.matmul(
                        bank[:, 0:TT], wsl[c % 2][:, j * 128:(j + 1) * 128],
                        ar[2][:, j * TT:(j + 1) * TT],
                        start=(j == 0), stop=(j == NC - 1)),
                        reads=[("wsl", c % 2), ("ar", 2, j)], writes=[K(bank)])
                p.op("act", lambda e, c=c, bank=bank: e.activation(
                    out=srb[:, c * TT:(c + 1) * TT], in_=bank[:, 0:TT], func=AF.Sigmoid),
                    reads=[K(bank)], writes=[("srb", c)])
            for c in range(NC):
                wslab(wk_h, c, c % 2, "dwk")
                kbank = env.psum()
                for j in range(NC):
                    p.op("pe", lambda e, c=c, j=j, kbank=kbank: e.matmul(
                        kbank[:, 0:TT], wsl[c % 2][:, j * 128:(j + 1) * 128],
                        ar[0][:, j * TT:(j + 1) * TT],
                        start=(j == 0), stop=(j == NC - 1)),
                        reads=[("wsl", c % 2), ("ar", 0, j)], writes=[K(kbank)])
                p.op("dve", lambda e, kbank=kbank: e.tensor_copy(kfw[:], kbank[:, 0:TT]),
                     reads=[K(kbank)], writes=[("kfw",)])
                if tt < 2 and c == 0:
                    dbg_tap(p, env, nc, "k", kfw[:], [("kfw",)], 1024)
                wslab(wv_h, c, c % 2, "dwk")
                vbank = env.psum()
                for j in range(NC):
                    p.op("pe", lambda e, c=c, j=j, vbank=vbank: e.matmul(
                        vbank[:, 0:TT], wsl[c % 2][:, j * 128:(j + 1) * 128],
                        ar[1][:, j * TT:(j + 1) * TT],
                        start=(j == 0), stop=(j == NC - 1)),
                        reads=[("wsl", c % 2), ("ar", 1, j)], writes=[K(vbank)])
                p.op("dve", lambda e, vbank=vbank: e.tensor_copy(vfw[:], vbank[:, 0:TT]),
                     reads=[K(vbank)], writes=[("vfw",)])
                mprev = cary[:, 16 + c:17 + c]
                scr = mcol[:, 3:4]
                p.op("dve", lambda e: e.reduce_max(mcol[:, 0:1], kfw[:], AX.X),
                     reads=[("kfw",)], writes=[("rmax",)])
                p.op("dve", lambda e, mprev=mprev: e.tensor_max(mcol[:, 1:2], mcol[:, 0:1], mprev),
                     reads=[("rmax",), ("cary", c)], writes=[("mnew",)])
                p.op("dve", lambda e, mprev=mprev, scr=scr: e.tensor_sub(scr, mprev, mcol[:, 1:2]),
                     reads=[("mnew",), ("cary", c)], writes=[("mscr",)])
                p.op("act", lambda e, scr=scr: e.activation(out=scr, in_=scr, func=AF.Exp),
                     reads=[("mscr",)], writes=[("mscr",)])
                p.op("dve", lambda e, c=c, scr=scr: e.tensor_mul(
                    numt[:, 0:1], cary[:, c:c + 1], scr),
                    reads=[("cary", c), ("mscr",)], writes=[("numt",)])
                p.op("dve", lambda e, c=c, scr=scr: e.tensor_mul(
                    dent[:, 0:1], cary[:, 8 + c:9 + c], scr),
                    reads=[("cary", c), ("mscr",)], writes=[("dent",)])
                p.op("dve", lambda e, c=c: e.tensor_copy(cary[:, 16 + c:17 + c], mcol[:, 1:2]),
                     reads=[("mnew",)], writes=[("cary", c)])
                p.op("dve", lambda e: e.tensor_scalar_mul(mcol[:, 2:3], mcol[:, 1:2], -1.0),
                     reads=[("mnew",)], writes=[("negm",)])
                p.op("act", lambda e: e.activation(out=s0[:], in_=kfw[:], func=AF.Exp,
                                                   bias=mcol[:, 2:3]),
                     reads=[("kfw",), ("negm",)], writes=[("s0d",)])
                p.op("dve", lambda e: e.tensor_mul(s1[:], s0[:], vfw[:]),
                     reads=[("s0d",), ("vfw",)], writes=[("s1d",)])
                p.op("dve", lambda e, c=c: e.tensor_scalar(ewb[:], ones512[:],
                                                           ewc[:, c:c + 1], None, ALU.mult),
                     reads=[("ones512",), ("ewc",)], writes=[("ewb",)])
                p.op("dve", lambda e: e.tensor_tensor_scan(numt[:, 1:513], ewb[:], s1[:],
                                                           numt[:, 0:1], ALU.mult, ALU.add),
                     reads=[("ewb",), ("s1d",), ("numt",)], writes=[("numt",)])
                p.op("dve", lambda e: e.tensor_tensor_scan(dent[:, 1:513], ewb[:], s0[:],
                                                           dent[:, 0:1], ALU.mult, ALU.add),
                     reads=[("ewb",), ("s0d",), ("dent",)], writes=[("dent",)])
                p.op("dve", lambda e, c=c: e.tensor_copy(cary[:, c:c + 1], numt[:, 512:513]),
                     reads=[("numt",)], writes=[("cary", c)])
                p.op("dve", lambda e, c=c: e.tensor_copy(cary[:, 8 + c:9 + c], dent[:, 512:513]),
                     reads=[("dent",)], writes=[("cary", c)])
                p.op("dve", lambda e, c=c: e.tensor_add(mcol[:, 2:3], mcol[:, 2:3], vc("tf", c)),
                     reads=[("negm",), ("vecs",)], writes=[("negm",)])
                p.op("act", lambda e: e.activation(out=s1[:], in_=kfw[:], func=AF.Exp,
                                                   bias=mcol[:, 2:3]),
                     reads=[("kfw",), ("negm",)], writes=[("s1d",)])
                p.op("dve", lambda e: e.tensor_mul(s2[:], s1[:], vfw[:]),
                     reads=[("s1d",), ("vfw",)], writes=[("s2d",)])
                p.op("dve", lambda e: e.tensor_add(s2[:], s2[:], numt[:, 0:512]),
                     reads=[("s2d",), ("numt",)], writes=[("s2d",)])
                p.op("dve", lambda e: e.scalar_tensor_tensor(s1[:], s1[:], 1e-38,
                                                             dent[:, 0:512], ALU.add, ALU.add),
                     reads=[("s1d",), ("dent",)], writes=[("s1d",)])
                p.op("dve", lambda e: e.reciprocal(s1[:], s1[:]),
                     reads=[("s1d",)], writes=[("s1d",)])
                p.op("dve", lambda e: e.tensor_mul(s2[:], s2[:], s1[:]),
                     reads=[("s2d",), ("s1d",)], writes=[("s2d",)])
                if tt < 2 and c == 0:
                    dbg_tap(p, env, nc, "y", s2[:], [("s2d",)], 1024)
                p.op("dve", lambda e, c=c: e.tensor_mul(ar[3][:, c * TT:(c + 1) * TT],
                                                        s2[:], srb[:, c * TT:(c + 1) * TT]),
                     reads=[("s2d",), ("srb", c)], writes=[("ar", 3, c)])
            for m in range(NC):
                wslab(wo_h, m, m % 2, "dwo")
                bank = env.psum()
                for j in range(NC):
                    p.op("pe", lambda e, m=m, j=j, bank=bank: e.matmul(
                        bank[:, 0:TT], wsl[m % 2][:, j * 128:(j + 1) * 128],
                        ar[3][:, j * TT:(j + 1) * TT],
                        start=(j == 0), stop=(j == NC - 1)),
                        reads=[("wsl", m % 2), ("ar", 3, j)], writes=[K(bank)])
                p.op("dve", lambda e, m=m, bank=bank: e.tensor_add(
                    x2b[:, m * TT:(m + 1) * TT], bank[:, 0:TT], x2b[:, m * TT:(m + 1) * TT]),
                    reads=[K(bank), ("x2b", m)], writes=[("x2b", m)])
            if tt < 2:
                dbg_tap(p, env, nc, "x2", x2b[:, 0:TT], [("x2b", 0)], 1024)
            for c in range(NC):
                p.op("dve", lambda e, c=c: e.tensor_copy(ht[:, c * 513:c * 513 + 1],
                                                         colsave[:, 8 + c:9 + c]),
                     reads=[("colsave", 1)], writes=[("ht", c)])
            ln_feat(p, env, [x2b[:, c * TT:(c + 1) * TT] for c in range(NC)],
                    [("x2b", c) for c in range(NC)], cast=True,
                    gamma=[vc("ln2w", c) for c in range(NC)],
                    beta=[vc("ln2b", c) for c in range(NC)], gkeys=[("vecs",)],
                    dst_aps=[ht[:, c * 513 + 1:c * 513 + 513] for c in range(NC)],
                    dst_keys=[("ht", c) for c in range(NC)], uid=f"ln2_{tt}")
            for c in range(NC):
                p.op("dve", lambda e, c=c: e.tensor_copy(colsave[:, 8 + c:9 + c],
                                                         ht[:, c * 513 + 512:c * 513 + 513]),
                     reads=[("ht", c)], writes=[("colsave", 1)])
            if tt < 2:
                dbg_tap(p, env, nc, "h2", ht[:, 1:513], [("ht", 0)], 1024)
            for s in range(2):
                for c in range(NC):
                    hcur = ht[:, c * 513 + 1 + s * 256:c * 513 + 1 + (s + 1) * 256]
                    xx2 = ht[:, c * 513 + s * 256:c * 513 + (s + 1) * 256]
                    p.op("dve", lambda e, c=c, hcur=hcur, xx2=xx2: e.tensor_sub(
                        tsub[:, 0:256], hcur, xx2),
                        reads=[("ht", c)], writes=[("tsub",)])
                    p.op("dve", lambda e, c=c, s=s, xx2=xx2: e.scalar_tensor_tensor(
                        ar[0][:, c * TT + s * 256:c * TT + (s + 1) * 256],
                        tsub[:, 0:256], vc("cmk", c), xx2, ALU.mult, ALU.add),
                        reads=[("tsub",), ("ht", c), ("vecs",)], writes=[("ar", 0, c)])
                    p.op("dve", lambda e, c=c, s=s, xx2=xx2: e.scalar_tensor_tensor(
                        ar[1][:, c * TT + s * 256:c * TT + (s + 1) * 256],
                        tsub[:, 0:256], vc("cmr", c), xx2, ALU.mult, ALU.add),
                        reads=[("tsub",), ("ht", c), ("vecs",)], writes=[("ar", 1, c)])
                    if tt < 2 and c == 0:
                        dbg_tap(p, env, nc, "xk2", ar[0][:, s * 256:(s + 1) * 256],
                                [("ar", 0, 0)], 1024)
                for m in range(32):
                    p.dma("pool", lambda e, m=m: e.dma_start(
                        out=wbig[m % 2][:, 0:1024].rearrange("p (k j) -> p k j", j=128),
                        in_=wfk_h[m]),
                        writes=[("wbig", m % 2)], sem=f"dwf{m % 2}")
                    bank = env.psum()
                    for j in range(NC):
                        p.op("pe", lambda e, m=m, j=j, s=s, bank=bank: e.matmul(
                            bank[:, 0:256], wbig[m % 2][:, j * 128:(j + 1) * 128],
                            ar[0][:, j * TT + s * 256:j * TT + (s + 1) * 256],
                            start=(j == 0), stop=(j == NC - 1)),
                            reads=[("wbig", m % 2), ("ar", 0, j)], writes=[K(bank)])
                    p.op("dve", lambda e, bank=bank: e.tensor_scalar_max(
                        s0[:, 0:256], bank[:, 0:256], 0.0),
                        reads=[K(bank)], writes=[("s0d",)])
                    if tt < 2 and m == 0:
                        dbg_tap(p, env, nc, "krl", s0[:, 0:256], [("s0d",)], 1024)
                    kfap = ar[2 + m // 16][:, (m % 16) * 256:(m % 16 + 1) * 256]
                    p.op("dve", lambda e, m=m, kfap=kfap: e.tensor_mul(
                        kfap, s0[:, 0:256], s0[:, 0:256]),
                        reads=[("s0d",)],
                        writes=[("kf", m), ("ar", 2 + m // 16, (m % 16) // 2)])
                    if tt < 2 and m == 0:
                        dbg_tap(p, env, nc, "kf", kfap, [("kf", 0)], 1024)
                for m2 in range(NC):
                    p.dma("pool", lambda e, m2=m2: e.dma_start(
                        out=wbig[m2 % 2][:].rearrange("p (k j) -> p k j", j=128),
                        in_=wfv_h[m2]),
                        writes=[("wbig", m2 % 2)], sem=f"dwf{m2 % 2}")
                    kvbank = env.psum()
                    for j in range(32):
                        kfap = ar[2 + j // 16][:, (j % 16) * 256:(j % 16 + 1) * 256]
                        p.op("pe", lambda e, m2=m2, j=j, kfap=kfap, kvbank=kvbank: e.matmul(
                            kvbank[:, 0:256], wbig[m2 % 2][:, j * 128:(j + 1) * 128], kfap,
                            start=(j == 0), stop=(j == 31)),
                            reads=[("wbig", m2 % 2), ("kf", j)], writes=[K(kvbank)])
                    p.dma("pool", lambda e, m2=m2: e.dma_start(
                        out=wsl[m2 % 2][:].rearrange("p (k j) -> p k j", j=128),
                        in_=wfr_h[m2]),
                        writes=[("wsl", m2 % 2)], sem=f"dwg{m2 % 2}")
                    frbank = env.psum()
                    for j in range(NC):
                        p.op("pe", lambda e, m2=m2, j=j, s=s, frbank=frbank: e.matmul(
                            frbank[:, 0:256], wsl[m2 % 2][:, j * 128:(j + 1) * 128],
                            ar[1][:, j * TT + s * 256:j * TT + (s + 1) * 256],
                            start=(j == 0), stop=(j == NC - 1)),
                            reads=[("wsl", m2 % 2), ("ar", 1, j)], writes=[K(frbank)])
                    p.op("act", lambda e, frbank=frbank: e.activation(
                        out=s1[:, 0:256], in_=frbank[:, 0:256], func=AF.Sigmoid),
                        reads=[K(frbank)], writes=[("s1d",)])
                    p.op("dve", lambda e, kvbank=kvbank: e.tensor_mul(
                        s2[:, 0:256], kvbank[:, 0:256], s1[:, 0:256]),
                        reads=[K(kvbank), ("s1d",)], writes=[("s2d",)])
                    if tt < 2 and m2 == 0:
                        dbg_tap(p, env, nc, "sk", s2[:, 0:256], [("s2d",)], 1024)
                    p.op("dve", lambda e, m2=m2, s=s: e.tensor_add(
                        s2[:, 0:256], s2[:, 0:256],
                        x2b[:, m2 * TT + s * 256:m2 * TT + (s + 1) * 256]),
                        reads=[("s2d",), ("x2b", m2)], writes=[("s2d",)])
                    p.op("dve", lambda e, m2=m2, s=s: e.tensor_add(
                        xt[:, m2 * TT + s * 256:m2 * TT + (s + 1) * 256],
                        xt[:, m2 * TT + s * 256:m2 * TT + (s + 1) * 256], s2[:, 0:256]),
                        reads=[("s2d",), ("xt", m2)], writes=[("xt", m2)])
            for c in range(NC):
                p.dma("sp", lambda e, c=c, tt=tt: e.dma_start(
                    out=y_h[c * 128:(c + 1) * 128, tt * TT:(tt + 1) * TT],
                    in_=xt[:, c * TT:(c + 1) * TT]),
                    reads=[("xt", c)], sem="dy")

        for c in range(NC):
            for i, key in ((0, 0), (1, 8), (2, 16)):
                p.op("dve", lambda e, c=c, i=i, key=key: e.tensor_copy(
                    stt_[:, key + c:key + c + 1], cary[:, key + c:key + c + 1]),
                    reads=[("cary", c)], writes=[("stt",)])
            p.op("dve", lambda e, c=c: e.tensor_copy(stt_[:, 24 + c:25 + c],
                                                     colsave[:, c:c + 1]),
                 reads=[("colsave", 0)], writes=[("stt",)])
            p.op("dve", lambda e, c=c: e.tensor_copy(stt_[:, 32 + c:33 + c],
                                                     colsave[:, 8 + c:9 + c]),
                 reads=[("colsave", 1)], writes=[("stt",)])
        p.dma("sp", lambda e: e.dma_start(out=st_h[:, :], in_=stt_[:]),
              reads=[("stt",)], sem="dy")

        if shared is None:
            p.finalize(es)
    return nc


def run_l1attnb(x_full, prm):
    nc = get_prog("l1attnb", build_l1attnb)
    xp = shard_x(x_full)
    vecs = np.concatenate([vec_pc(prm[n]) for n in
                           ["ln0_w", "ln0_b", "ln1_w", "ln1_b", "ln2_w", "ln2_b",
                            "time_decay", "time_first", "tmk", "tmv", "tmr",
                            "cmk", "cmr"]], axis=1).astype(np.float32)
    wts = {"wk": w_pre(np.ascontiguousarray(prm["Wk"].T)),
           "wv": w_pre(np.ascontiguousarray(prm["Wv"].T)),
           "wr": w_pre(np.ascontiguousarray(prm["Wr"].T)),
           "wo": w_pre(np.ascontiguousarray(prm["Wo"].T)),
           "wfk": w_pre(np.ascontiguousarray(prm["Wfk"].T)),
           "wfv": w_pre(np.ascontiguousarray(prm["Wfv"].T)),
           "wfr": w_pre(np.ascontiguousarray(prm["Wfr"].T))}
    zero_ci = np.zeros((128, 24), np.float32)
    zero_ci[:, 16:24] = -1e38
    zero_hh = np.zeros((128, 16), np.float32)
    in_maps = [dict(x=xp[c], vecs=vecs, ci=zero_ci, hhal=zero_hh, **wts)
               for c in range(NCORES)]
    res1 = run_bass_kernel_spmd(nc, in_maps, core_ids=list(range(NCORES)))
    in_maps2 = []
    for c in range(NCORES):
        if c % 2 == 1:
            st = np.asarray(res1.results[c - 1]["st"])
            ci = np.ascontiguousarray(st[:, 0:24])
            hh = np.ascontiguousarray(st[:, 24:40])
        else:
            ci, hh = zero_ci, zero_hh
        in_maps2.append(dict(x=xp[c], vecs=vecs, ci=ci, hhal=hh, **wts))
    res2 = run_bass_kernel_spmd(nc, in_maps2, core_ids=list(range(NCORES)))
    return unshard_x([np.asarray(res2.results[c]["y"]) for c in range(NCORES)])





def build_l0ffn_layer(first, shared=None, x_ap=None, y_ap=None, xres_ap=None, pfx=""):
    if shared is None:
        nc = bass.Bass("TRN2", target_bir_lowering=False)
        p0 = Prog(nc)
    else:
        nc, p0 = shared

    def din(name, shape):
        return nc.dram_tensor(pfx + name, list(shape), F32, kind="ExternalInput").ap()

    x_h = x_ap if x_ap is not None else din("x", [D, TCORE])
    xtail_h = din("xtail", [D, 513])
    xres_h = xres_ap if xres_ap is not None else din("xres", [D, TCORE])
    pm_h = din("pm", [128, 1])
    wqk_h = din("wqk", [8, 128, 8, 128])
    wv_h = din("wv", [128, 8, 512])
    qsc_h = din("qsc", [128, 2])
    wo_h = din("wo", [8, 128, 4, 128])
    lgb_h = din("lgb", [128, 16])
    w1_h = din("w1", [44, 128, 8, 128])
    b1_h = din("b1", [128, 44])
    w2_h = din("w2", [8, 128, 22, 128])
    b2_h = din("b2", [128, 8])
    rot_h = din("rot", [128, 4 * (513 + TCORE)])
    swp_h = din("swp", [128, 128])
    e2a_h = din("e2a", [128, 2])
    e2b_h = din("e2b", [2, 128])
    trim_h = din("trim", [128, 256])
    y_h = y_ap if y_ap is not None else nc.dram_tensor(
        pfx + "y", [D, TCORE], F32, kind="ExternalOutput").ap()

    M = TT + 1
    with contextlib.ExitStack() as es:
        p = p0
        env = Env(nc, p, es)
        env.const(1e-5)
        alloc_ln_tmps(nc, env, es)
        xi = [sbuf(nc, es, f"xi{i}", (128, NC * M), BF16) for i in range(2)]
        xt = sbuf(nc, es, "xt", (128, NC * TT), F32)
        qb = sbuf(nc, es, "qb", (128, 4 * TT), BF16)
        kb = [sbuf(nc, es, f"kb{i}", (128, 4 * TT), BF16) for i in range(2)]
        vb = [sbuf(nc, es, f"vb{i}", (128, 4 * TT), BF16) for i in range(2)]
        ob = sbuf(nc, es, "ob", (128, 4 * TT), BF16)
        hbn = sbuf(nc, es, "hbn", (128, 22 * TT), BF16)
        rots = sbuf(nc, es, "rots", (128, 4 * (513 + TCORE)), BF16)
        swps = sbuf(nc, es, "swps", (128, 128), BF16)
        e2a = sbuf(nc, es, "e2as", (128, 2), BF16)
        e2b = sbuf(nc, es, "e2bs", (2, 128), F32)
        trims = sbuf(nc, es, "trims", (128, 256), F32)
        wvs = sbuf(nc, es, "wvs", (128, 8 * 512), BF16)
        qscs = sbuf(nc, es, "qscs", (128, 2), F32)
        lgbs = sbuf(nc, es, "lgbs", (128, 16), F32)
        b1s = sbuf(nc, es, "b1s", (128, 44), F32)
        b2s = sbuf(nc, es, "b2s", (128, 8), F32)
        pms = sbuf(nc, es, "pms", (128, 1), F32)
        nrm2 = sbuf(nc, es, "nrm2", (2, TT), F32)
        sq = sbuf(nc, es, "sq", (128, TT), BF16)
        gt = sbuf(nc, es, "gt", (128, TT), BF16)
        pb = sbuf(nc, es, "pb", (128, 5 * 128), BF16)
        ptb = sbuf(nc, es, "ptb", (128, 5 * 128), BF16)
        mcol = sbuf(nc, es, "mcol", (128, 6), F32)
        midc = sbuf(nc, es, "midc", (128, 8), BF16)
        halo1 = sbuf(nc, es, "halo1", (128, 8), F32)
        wsl = [sbuf(nc, es, f"wsl{i}", (128, 8 * 128), BF16) for i in range(2)]
        w2sl = [sbuf(nc, es, f"w2sl{i}", (128, 22 * 128), BF16) for i in range(2)]
        make_ident(p, env, es)

        for nm, dst, hh in (("rots", rots, rot_h), ("swps", swps, swp_h),
                            ("e2a", e2a, e2a_h), ("e2b", e2b, e2b_h),
                            ("trims", trims, trim_h), ("qscs", qscs, qsc_h),
                            ("lgbs", lgbs, lgb_h), ("b1s", b1s, b1_h),
                            ("b2s", b2s, b2_h), ("pms", pms, pm_h)):
            p.dma("pool", lambda e, dst=dst, hh=hh: e.dma_start(out=dst[:], in_=hh[:, :]),
                  writes=[(nm,)], sem="dc" + nm)
        p.dma("pool", lambda e: e.dma_start(
            out=wvs[:].rearrange("p (c j) -> p c j", j=512), in_=wv_h[:, :, :]),
            writes=[("wvs",)], sem="dcwv")

        RSZ = 513 + TCORE
        ROQ, RSQ, ROK, RSK = 0, RSZ, 2 * RSZ, 3 * RSZ

        def keyx(i, c):
            return ("xi", i, c)

        def load_tile(tt, i, iprev):
            if tt >= 0:
                for c in range(NC):
                    p.dma("sp", lambda e, c=c, tt=tt: e.dma_start(
                        out=xt[:, c * TT:(c + 1) * TT],
                        in_=x_h[c * 128:(c + 1) * 128, tt * TT:(tt + 1) * TT]),
                        writes=[("xt", c)], sem=f"dx{c}")
            else:
                for c in range(NC):
                    p.dma("sp", lambda e, c=c: e.dma_start(
                        out=xt[:, c * TT:(c + 1) * TT],
                        in_=xtail_h[c * 128:(c + 1) * 128, 1:513]),
                        writes=[("xt", c)], sem=f"dx{c}")
            if first:
                ln_feat(p, env, [xt[:, c * TT:(c + 1) * TT] for c in range(NC)],
                        [("xt", c) for c in range(NC)], cast=True,
                        dst_aps=[xi[i][:, c * M + 1:c * M + M] for c in range(NC)],
                        dst_keys=[keyx(i, c) for c in range(NC)], uid=f"lt{tt}")
            else:
                for c in range(NC):
                    p.op("dve", lambda e, c=c, i=i: e.tensor_copy(
                        xi[i][:, c * M + 1:c * M + M], xt[:, c * TT:(c + 1) * TT]),
                        reads=[("xt", c)], writes=[keyx(i, c)])
            if tt == -1:
                for c in range(NC):
                    p.dma("pool", lambda e, c=c: e.dma_start(
                        out=halo1[:, c:c + 1], in_=xtail_h[c * 128:(c + 1) * 128, 0:1],
                        allow_slow_non_contiguous=True),
                        writes=[("halo1",)], sem="dh1")
                if first:
                    col_ln(p, env, halo1, ("halo1",))
                for c in range(NC):
                    p.op("dve", lambda e, c=c, i=i: e.tensor_copy(
                        xi[i][:, c * M:c * M + 1], halo1[:, c:c + 1]),
                        reads=[("halo1",)], writes=[keyx(i, c)])
            else:
                for c in range(NC):
                    p.op("dve", lambda e, c=c, i=i, iprev=iprev: e.tensor_copy(
                        xi[i][:, c * M:c * M + 1],
                        xi[iprev][:, c * M + M - 1:c * M + M]),
                        reads=[keyx(iprev, c)], writes=[keyx(i, c)])

        def shifted(i, c):
            if c < 4:
                return xi[i][:, c * M + 1:c * M + M]
            return xi[i][:, c * M:c * M + M - 1]

        def qkv_for(tt, i, with_q):
            rng = range(0, 8) if with_q else range(4, 8)
            for m in rng:
                slot = wsl[m % 2]
                p.dma("pool", lambda e, m=m, slot=slot: e.dma_start(
                    out=slot[:].rearrange("p (k j) -> p k j", j=128), in_=wqk_h[m]),
                    writes=[("wsl", m % 2)], sem=f"dwq{m % 2}")
                bank = env.psum()
                for j in range(NC):
                    p.op("pe", lambda e, m=m, j=j, i=i, slot=slot, bank=bank: e.matmul(
                        bank[:, 0:TT], slot[:, j * 128:(j + 1) * 128], shifted(i, j),
                        start=(j == 0), stop=(j == NC - 1)),
                        reads=[("wsl", m % 2), keyx(i, j)], writes=[K(bank)])
                if m < 4:
                    p.op("dve", lambda e, m=m, bank=bank: e.tensor_copy(
                        qb[:, m * TT:(m + 1) * TT], bank[:, 0:TT]),
                        reads=[K(bank)], writes=[("qb", m)])
                else:
                    p.op("dve", lambda e, m=m, i=i, bank=bank: e.tensor_copy(
                        kb[i][:, (m - 4) * TT:(m - 3) * TT], bank[:, 0:TT]),
                        reads=[K(bank)], writes=[("kb", i, m - 4)])
            for t4 in range(4):
                bank = env.psum()
                for j in range(NC):
                    p.op("pe", lambda e, t4=t4, j=j, i=i, bank=bank: e.matmul(
                        bank[:, 0:512], shifted(i, j)[:, t4 * 128:(t4 + 1) * 128],
                        wvs[:, j * 512:(j + 1) * 512],
                        start=(j == 0), stop=(j == NC - 1)),
                        reads=[keyx(i, j), ("wvs",)], writes=[K(bank)])
                p.op("dve", lambda e, t4=t4, i=i, bank=bank: e.tensor_copy(
                    vb[i][:, t4 * 512:(t4 + 1) * 512], bank[:, 0:512]),
                    reads=[K(bank)], writes=[("vb", i, t4)])

        def l2rot(dst, dkeyf, roff, soff, sccol, sckey, tt):
            base = 513 + tt * TT
            for c in range(4):
                ap = dst[:, c * TT:(c + 1) * TT]
                dk = dkeyf(c)
                p.op("dve", lambda e, ap=ap: e.tensor_mul(sq[:], ap, ap),
                     reads=[dk], writes=[("sq",)])
                nb = env.psum()
                p.op("pe", lambda e, nb=nb: e.matmul(nb[0:2, 0:TT], e2a[:], sq[:],
                                                     start=True, stop=True),
                     reads=[("sq",), ("e2a",)], writes=[K(nb)])
                p.op("act", lambda e, nb=nb: e.activation(out=nrm2[:, 0:TT],
                                                          in_=nb[0:2, 0:TT], func=AF.Sqrt),
                     reads=[K(nb)], writes=[("nrm2",)])
                p.op("dve", lambda e: e.tensor_scalar_max(nrm2[:, 0:TT], nrm2[:, 0:TT], 1e-12),
                     reads=[("nrm2",)], writes=[("nrm2",)])
                p.op("dve", lambda e: e.reciprocal(nrm2[:, 0:TT], nrm2[:, 0:TT]),
                     reads=[("nrm2",)], writes=[("nrm2",)])
                nb2 = env.psum()
                p.op("pe", lambda e, nb2=nb2: e.matmul(nb2[:, 0:TT], e2b[:], nrm2[:, 0:TT],
                                                       start=True, stop=True),
                     reads=[("nrm2",), ("e2b",)], writes=[K(nb2)])
                p.op("dve", lambda e, ap=ap, nb2=nb2: e.tensor_mul(ap, ap, nb2[:, 0:TT]),
                     reads=[dk, K(nb2)], writes=[dk])
                p.op("dve", lambda e, ap=ap, sccol=sccol: e.tensor_scalar(
                    ap, ap, sccol, None, ALU.mult),
                    reads=[dk, sckey], writes=[dk])
                sw = env.psum()
                p.op("pe", lambda e, ap=ap, sw=sw: e.matmul(sw[:, 0:TT], swps[:], ap,
                                                            start=True, stop=True),
                     reads=[dk, ("swps",)], writes=[K(sw)])
                cs = rots[:, roff + base:roff + base + TT]
                ss = rots[:, soff + base:soff + base + TT]
                p.op("dve", lambda e, ap=ap, cs=cs: e.tensor_mul(ap, ap, cs),
                     reads=[dk, ("rots",)], writes=[dk])
                p.op("dve", lambda e, sw=sw, ss=ss: e.tensor_mul(gt[:], sw[:, 0:TT], ss),
                     reads=[K(sw), ("rots",)], writes=[("gt",)])
                p.op("dve", lambda e, ap=ap: e.tensor_add(ap, ap, gt[:]),
                     reads=[dk, ("gt",)], writes=[dk])

        def attention(tt, i, iprev):
            for h in range(8):
                hc, hr = h // 2, 64 * (h % 2)
                for qt in range(4):
                    np_ = 4 - qt
                    nc_ = qt + 1
                    qs = qb[:, hc * TT + qt * 128:hc * TT + (qt + 1) * 128]
                    sa = env.psum()
                    p.op("pe", lambda e, h=h, hc=hc, hr=hr, qt=qt, np_=np_, i=i,
                         iprev=iprev, sa=sa, qs=qs: e.matmul(
                        sa[:, 0:np_ * 128], qs[hr:hr + 64, :],
                        kb[iprev][hr:hr + 64, hc * TT + qt * 128:hc * TT + 512],
                        start=True, stop=True),
                        reads=[("qb", hc), ("kb", iprev, hc)], writes=[K(sa)])
                    sb = env.psum()
                    p.op("pe", lambda e, h=h, hc=hc, hr=hr, qt=qt, nc_=nc_, i=i,
                         sb=sb, qs=qs: e.matmul(
                        sb[:, 0:nc_ * 128], qs[hr:hr + 64, :],
                        kb[i][hr:hr + 64, hc * TT:hc * TT + nc_ * 128],
                        start=True, stop=True),
                        reads=[("qb", hc), ("kb", i, hc)], writes=[K(sb)])
                    p.op("dve", lambda e, sa=sa: e.tensor_add(sa[:, 0:128], sa[:, 0:128],
                                                              trims[:, 0:128]),
                         reads=[K(sa), ("trims",)], writes=[K(sa)])
                    p.op("dve", lambda e, sb=sb, qt=qt: e.tensor_add(
                        sb[:, qt * 128:(qt + 1) * 128], sb[:, qt * 128:(qt + 1) * 128],
                        trims[:, 128:256]),
                        reads=[K(sb), ("trims",)], writes=[K(sb)])
                    if tt == 0:
                        p.op("dve", lambda e, sa=sa, np_=np_: e.tensor_scalar(
                            sa[:, 0:np_ * 128], sa[:, 0:np_ * 128], pms[:, 0:1],
                            None, ALU.add),
                            reads=[K(sa), ("pms",)], writes=[K(sa)])
                    p.op("dve", lambda e, sa=sa, np_=np_: e.reduce_max(
                        mcol[:, 0:1], sa[:, 0:np_ * 128], AX.X),
                        reads=[K(sa)], writes=[("mc0",)])
                    p.op("dve", lambda e, sb=sb, nc_=nc_: e.reduce_max(
                        mcol[:, 1:2], sb[:, 0:nc_ * 128], AX.X),
                        reads=[K(sb)], writes=[("mc1",)])
                    p.op("dve", lambda e: e.tensor_max(mcol[:, 2:3], mcol[:, 0:1],
                                                       mcol[:, 1:2]),
                         reads=[("mc0",), ("mc1",)], writes=[("mc2",)])
                    p.op("dve", lambda e: e.tensor_scalar_mul(mcol[:, 3:4], mcol[:, 2:3], -1.0),
                         reads=[("mc2",)], writes=[("mc3",)])
                    p.op("act", lambda e, sa=sa, np_=np_: e.activation(
                        out=pb[:, 0:np_ * 128], in_=sa[:, 0:np_ * 128], func=AF.Exp,
                        bias=mcol[:, 3:4], accum_out=mcol[:, 4:5]),
                        reads=[K(sa), ("mc3",)], writes=[("pb",), ("mc4",)])
                    p.op("act", lambda e, sb=sb, np_=np_, nc_=nc_: e.activation(
                        out=pb[:, np_ * 128:640], in_=sb[:, 0:nc_ * 128], func=AF.Exp,
                        bias=mcol[:, 3:4], accum_out=mcol[:, 5:6]),
                        reads=[K(sb), ("mc3",)], writes=[("pb",), ("mc5",)])
                    p.op("dve", lambda e: e.tensor_add(mcol[:, 4:5], mcol[:, 4:5],
                                                       mcol[:, 5:6]),
                         reads=[("mc4",), ("mc5",)], writes=[("mc4",)])
                    p.op("dve", lambda e: e.reciprocal(mcol[:, 5:6], mcol[:, 4:5]),
                         reads=[("mc4",)], writes=[("mc5",)])
                    p.op("dve", lambda e: e.tensor_scalar(pb[:], pb[:], mcol[:, 5:6],
                                                          None, ALU.mult),
                         reads=[("pb",), ("mc5",)], writes=[("pb",)])
                    obank = env.psum()
                    for kc in range(5):
                        p.op("pe", lambda e, kc=kc: e.transpose(
                            env.ps_bf[:, 0:128], pb[:, kc * 128:(kc + 1) * 128],
                            env.identb[:]),
                            reads=[("pb",), ("identb",)], writes=[K(env.ps_bf)])
                        p.op("dve", lambda e, kc=kc: e.tensor_copy(
                            ptb[:, kc * 128:(kc + 1) * 128], env.ps_bf[:, 0:128]),
                            reads=[K(env.ps_bf)], writes=[("ptb", kc)])
                    for kc in range(5):
                        if kc < np_:
                            vi, vt4 = iprev, qt + kc
                        else:
                            vi, vt4 = i, kc - np_
                        p.op("pe", lambda e, kc=kc, vi=vi, vt4=vt4, h=h, hr=hr,
                             obank=obank: e.matmul(
                            obank[hr:hr + 64, 0:128],
                            vb[vi][:, vt4 * 512 + h * 64:vt4 * 512 + (h + 1) * 64],
                            ptb[:, kc * 128:(kc + 1) * 128],
                            start=(kc == 0), stop=(kc == 4)),
                            reads=[("vb", vi, vt4), ("ptb", kc)], writes=[K(obank)])
                    p.op("dve", lambda e, hc=hc, hr=hr, qt=qt, obank=obank: e.tensor_copy(
                        ob[hr:hr + 64, hc * TT + qt * 128:hc * TT + (qt + 1) * 128],
                        obank[hr:hr + 64, 0:128]),
                        reads=[K(obank)], writes=[("ob", hc)])

        def wo_ffn(tt, i):
            def epo(m, bank):
                p.op("dve", lambda e, m=m, i=i, bank=bank: e.tensor_add(
                    xi[i][:, m * M + 1:m * M + M], bank[:, 0:TT],
                    xi[i][:, m * M + 1:m * M + M]),
                    reads=[K(bank), keyx(i, m)], writes=[keyx(i, m)])
            stream_gemm(p, env, wpre_h=wo_h, nk=4, nm=8,
                        rhs_ap=lambda k: ob[:, k * TT:(k + 1) * TT],
                        rhs_keys=lambda k: [("ob", k)],
                        out_fn=epo, slots=wsl, slot_base="wsl", sem="dwo", tag=f"wo{tt}")
            for c in range(4, NC):
                p.op("dve", lambda e, c=c, i=i: e.tensor_copy(
                    xi[i][:, c * M:c * M + 1], midc[:, c:c + 1]),
                    reads=[("midc",)], writes=[keyx(i, c)])
            for c in range(NC):
                p.op("dve", lambda e, c=c, i=i: e.tensor_copy(
                    midc[:, c:c + 1], xi[i][:, c * M + M - 1:c * M + M]),
                    reads=[keyx(i, c)], writes=[("midc",)])
            lnaps, lnkeys = ln_feat(p, env, [shifted(i, c) for c in range(NC)],
                                    [keyx(i, c) for c in range(NC)], cast=True,
                                    gamma=[lgbs[:, c:c + 1] for c in range(NC)],
                                    beta=[lgbs[:, 8 + c:9 + c] for c in range(NC)],
                                    gkeys=[("lgbs",)], uid=f"lf{tt}")
            for k2 in range(22):
                slots = wsl
                banks = []
                for half in range(2):
                    m = 2 * k2 + half
                    slot = slots[half]
                    p.dma("pool", lambda e, m=m, slot=slot: e.dma_start(
                        out=slot[:].rearrange("p (k j) -> p k j", j=128), in_=w1_h[m]),
                        writes=[("wsl", half)], sem=f"dw1{half}")
                    bank = env.psum()
                    for j in range(NC):
                        p.op("pe", lambda e, m=m, j=j, slot=slot, bank=bank,
                             lnaps=lnaps: e.matmul(
                            bank[:, 0:TT], slot[:, j * 128:(j + 1) * 128], lnaps[j],
                            start=(j == 0), stop=(j == NC - 1)),
                            reads=[("wsl", half), lnkeys[j]], writes=[K(bank)])
                    banks.append(bank)
                bankA, bankB = banks
                p.op("act", lambda e, k2=k2, bankB=bankB: e.activation(
                    out=gt[:], in_=bankB[:, 0:TT], func=AF.Gelu,
                    bias=b1s[:, 2 * k2 + 1:2 * k2 + 2]),
                    reads=[K(bankB), ("b1s",)], writes=[("gt",)])
                p.op("dve", lambda e, k2=k2, bankA=bankA: e.scalar_tensor_tensor(
                    hbn[:, k2 * TT:(k2 + 1) * TT], bankA[:, 0:TT],
                    b1s[:, 2 * k2:2 * k2 + 1], gt[:], ALU.add, ALU.mult),
                    reads=[K(bankA), ("b1s",), ("gt",)], writes=[("hbn", k2)])
            def ep2(m2, bank):
                p.op("dve", lambda e, m2=m2, i=i, bank=bank: e.scalar_tensor_tensor(
                    xi[i][:, m2 * M + 1:m2 * M + M], bank[:, 0:TT], b2s[:, m2:m2 + 1],
                    xi[i][:, m2 * M + 1:m2 * M + M], ALU.add, ALU.add),
                    reads=[K(bank), ("b2s",), keyx(i, m2)], writes=[keyx(i, m2)])
            stream_gemm(p, env, wpre_h=w2_h, nk=22, nm=8,
                        rhs_ap=lambda k: hbn[:, k * TT:(k + 1) * TT],
                        rhs_keys=lambda k: [("hbn", k)],
                        out_fn=ep2, slots=w2sl, slot_base="w2sl", sem="dw2", tag=f"w2{tt}")

        def store_tile(tt, i):
            for c in range(NC):
                if first:
                    p.op("dve", lambda e, c=c, i=i: e.tensor_copy(
                        xt[:, c * TT:(c + 1) * TT], xi[i][:, c * M + 1:c * M + M]),
                        reads=[keyx(i, c)], writes=[("xt", c)])
                else:
                    p.dma("pool", lambda e, c=c, tt=tt: e.dma_start(
                        out=xt[:, c * TT:(c + 1) * TT],
                        in_=xres_h[c * 128:(c + 1) * 128, tt * TT:(tt + 1) * TT]),
                        writes=[("xt", c)], sem=f"dr{c}")
                    p.op("dve", lambda e, c=c, i=i: e.tensor_add(
                        xt[:, c * TT:(c + 1) * TT], xt[:, c * TT:(c + 1) * TT],
                        xi[i][:, c * M + 1:c * M + M]),
                        reads=[keyx(i, c), ("xt", c)], writes=[("xt", c)])
                p.dma("sp", lambda e, c=c, tt=tt: e.dma_start(
                    out=y_h[c * 128:(c + 1) * 128, tt * TT:(tt + 1) * TT],
                    in_=xt[:, c * TT:(c + 1) * TT]),
                    reads=[("xt", c)], sem="dy")

        load_tile(-1, 1, 0)
        qkv_for(-1, 1, with_q=False)
        l2rot(kb[1], lambda c: ("kb", 1, c), ROK, RSK, qscs[:, 1:2], ("qscs",), -1)
        for tt in range(NTT):
            i, iprev = tt % 2, 1 - tt % 2
            load_tile(tt, i, iprev)
            qkv_for(tt, i, with_q=True)
            l2rot(qb, lambda c: ("qb", c), ROQ, RSQ, qscs[:, 0:1], ("qscs",), tt)
            l2rot(kb[i], lambda c, i=i: ("kb", i, c), ROK, RSK, qscs[:, 1:2], ("qscs",), tt)
            attention(tt, i, iprev)
            wo_ffn(tt, i)
            store_tile(tt, i)

        if shared is None:
            p.finalize(es)
    return nc


def col_ln(p, env, col, ckey, eps=1e-5):
    stat = env.ps_stat
    mc = env.tmp_rows
    p.op("pe", lambda e: e.matmul(stat[0:1, 0:8], env.ones_colf[:], col[:],
                                  start=True, stop=True),
         reads=[ckey, ("ones_colf",)], writes=[K(stat)])
    p.op("dve", lambda e: e.tensor_reduce(mc[0:1, 0:1], stat[0:1, 0:8], AX.X, ALU.add),
         reads=[K(stat)], writes=[("clnA",)])
    p.op("dve", lambda e: e.tensor_mul(env.tmp_x2b[:, 0:8], col[:], col[:]),
         reads=[ckey], writes=[("x2b",)])
    st2 = env.psum()
    p.op("pe", lambda e, st2=st2: e.matmul(st2[0:1, 0:8], env.ones_col[:],
                                           env.tmp_x2b[:, 0:8], start=True, stop=True),
         reads=[("x2b",), ("ones_col",)], writes=[K(st2)])
    p.op("dve", lambda e, st2=st2: e.tensor_reduce(mc[0:1, 1:2], st2[0:1, 0:8],
                                                   AX.X, ALU.add),
         reads=[K(st2)], writes=[("clnB",)])
    p.op("dve", lambda e: e.tensor_scalar_mul(mc[0:1, 0:1], mc[0:1, 0:1], 1.0 / D),
         reads=[("clnA",)], writes=[("clnA",)])
    p.op("dve", lambda e: e.tensor_mul(mc[0:1, 2:3], mc[0:1, 0:1], mc[0:1, 0:1]),
         reads=[("clnA",)], writes=[("clnC",)])
    p.op("dve", lambda e: e.scalar_tensor_tensor(mc[0:1, 1:2], mc[0:1, 1:2], 1.0 / D,
                                                 mc[0:1, 2:3], ALU.mult, ALU.subtract),
         reads=[("clnB",), ("clnC",)], writes=[("clnB",)])
    epsap = env.const(eps)
    p.op("act", lambda e, epsap=epsap: e.activation(out=mc[0:1, 1:2], in_=mc[0:1, 1:2],
                                                    func=AF.Sqrt, bias=epsap[0:1, :]),
         reads=[("clnB",), env.const_key(eps)], writes=[("clnB",)])
    p.op("dve", lambda e: e.reciprocal(mc[0:1, 1:2], mc[0:1, 1:2]),
         reads=[("clnB",)], writes=[("clnB",)])
    p.op("dve", lambda e: e.tensor_scalar_mul(mc[0:1, 0:1], mc[0:1, 0:1], -1.0),
         reads=[("clnA",)], writes=[("clnA",)])
    bcb = env.psum()
    p.op("pe", lambda e, bcb=bcb: e.matmul(bcb[:, 0:1], env.ones1f[:], mc[0:1, 0:1],
                                           start=True, stop=True),
         reads=[("clnA",), ("ones1f",)], writes=[K(bcb)])
    p.op("pe", lambda e, bcb=bcb: e.matmul(bcb[:, 1:2], env.ones1f[:], mc[0:1, 1:2],
                                           start=True, stop=True),
         reads=[("clnB",), ("ones1f",)], writes=[K(bcb)])
    p.op("dve", lambda e, bcb=bcb: e.tensor_scalar(col[:], col[:], bcb[:, 0:1], None,
                                                   ALU.add),
         reads=[ckey, K(bcb)], writes=[ckey])
    p.op("dve", lambda e, bcb=bcb: e.tensor_scalar(col[:], col[:], bcb[:, 1:2], None,
                                                   ALU.mult),
         reads=[ckey, K(bcb)], writes=[ckey])





def build_mamba_layer(first, shared=None, x_ap=None, y_ap=None, xres_ap=None, pfx=""):
    if shared is None:
        nc = bass.Bass("TRN2", target_bir_lowering=False)
        p0 = Prog(nc)
    else:
        nc, p0 = shared

    def din(name, shape):
        return nc.dram_tensor(pfx + name, list(shape), F32, kind="ExternalInput").ap()

    x_h = x_ap if x_ap is not None else din("x", [D, TCORE])
    xhalo_h = din("xhalo", [D, 4])
    xres_h = xres_ap if xres_ap is not None else din("xres", [D, TCORE])
    lnwb_h = din("lnwb", [128, 16])
    inw_h = din("inw", [34, 128, 8, 128])
    convw_h = din("convw", [128, 68])
    convb_h = din("convb", [128, 17])
    hvec_h = din("hvec", [32, 2])
    drms_h = din("drms", [128, 32])
    outw_h = din("outw", [8, 128, 16, 128])
    sin_h = din("sin", [64, 2048])
    ehf_h = din("ehf", [32, 32 * 128])
    h2_h = din("h2c", [32, 2048])
    maskt_h = din("maskt", [128, 128])
    hmask_h = din("hmask", [128, 1])
    shf_h = din("shf", [128, 64])
    y_h = y_ap if y_ap is not None else nc.dram_tensor(
        pfx + "y", [D, TCORE], F32, kind="ExternalOutput").ap()
    st_h = nc.dram_tensor(pfx + "st", [64, 2048], F32, kind="ExternalOutput").ap()

    NH, LC = 32, 128
    NCH = 4
    MB = 515
    with contextlib.ExitStack() as es:
        p = p0
        env = Env(nc, p, es)
        env.const(1e-5)
        env.const(1.0)
        alloc_ln_tmps(nc, env, es)
        make_ident(p, env, es)
        xt = sbuf(nc, es, "xt", (128, NC * TT), F32)
        h0b = sbuf(nc, es, "h0b", (128, NC * TT), BF16)
        zb = sbuf(nc, es, "zb", (128, 16 * TT), BF16)
        xbc = sbuf(nc, es, "xbc", (128, 17 * MB), BF16)
        xh = sbuf(nc, es, "xh", (128, 16 * TT), BF16)
        yb = sbuf(nc, es, "yb", (128, 16 * TT), BF16)
        dtr = sbuf(nc, es, "dtr", (32, TT), F32)
        cumA = sbuf(nc, es, "cumA", (32, TT), F32)
        grow = sbuf(nc, es, "grow", (32, TT), F32)
        wS = sbuf(nc, es, "wS", (32, TT), F32)
        ecA = sbuf(nc, es, "ecA", (32, TT), F32)
        hv = sbuf(nc, es, "hv", (32, 4), F32)
        S = sbuf(nc, es, "S", (64, 2048), F32)
        Sb = sbuf(nc, es, "Sb", (64, 2048), BF16)
        lnwb = sbuf(nc, es, "lnwbs", (128, 16), F32)
        convw = sbuf(nc, es, "cw", (128, 68), F32)
        convb = sbuf(nc, es, "cb", (128, 17), F32)
        drms = sbuf(nc, es, "drmss", (128, 32), F32)
        ehf = sbuf(nc, es, "ehfs", (32, 32 * 128), F32)
        h2c = sbuf(nc, es, "h2cs", (32, 2048), F32)
        h2s = sbuf(nc, es, "h2s", (32, 128), F32)
        maskt = sbuf(nc, es, "maskts", (128, 128), F32)
        hmask = sbuf(nc, es, "hmasks", (128, 1), F32)
        shf = sbuf(nc, es, "shfs", (128, 64), BF16)
        c0b = sbuf(nc, es, "c0b", (64, TT), BF16)
        cbt = sbuf(nc, es, "cbt", (128, 128), BF16)
        et4 = sbuf(nc, es, "et4", (128, 512), BF16)
        mpt = sbuf(nc, es, "mpt", (128, 4 * 128), BF16)
        cdec = sbuf(nc, es, "cdec", (64, 128), BF16)
        xhT = sbuf(nc, es, "xhT", (128, 16 * 128), BF16)
        bt = sbuf(nc, es, "bt", (128, 64), BF16)
        xwt = sbuf(nc, es, "xwt", (128, 128), BF16)
        scv = sbuf(nc, es, "scv", (128, TT), BF16)
        wsl = [sbuf(nc, es, f"wsl{i}", (128, 8 * 128), BF16) for i in range(2)]
        wosl = [sbuf(nc, es, f"wosl{i}", (128, 16 * 128), BF16) for i in range(2)]
        halo3 = sbuf(nc, es, "halo3", (128, 24), F32)
        xrt = sbuf(nc, es, "xrt", (128, TT), F32)

        for nm, dst, hh in (("lnwb", lnwb, lnwb_h), ("cw", convw, convw_h),
                            ("cb", convb, convb_h), ("drms", drms, drms_h),
                            ("ehf", ehf, ehf_h), ("h2c", h2c, h2_h),
                            ("maskt", maskt, maskt_h), ("hmask", hmask, hmask_h),
                            ("shf", shf, shf_h)):
            p.dma("pool", lambda e, dst=dst, hh=hh: e.dma_start(out=dst[:], in_=hh[:, :]),
                  writes=[(nm,)], sem="dc" + nm)
        p.dma("sp", lambda e: e.dma_start(out=hv[:, 0:2], in_=hvec_h[:, :]),
              writes=[("hv",)], sem="dchv")
        p.dma("sp", lambda e: e.dma_start(out=S[:], in_=sin_h[:, :]),
              writes=[("S",)], sem="dcS")
        p.op("act", lambda e: e.activation(out=hv[:, 2:3], in_=hv[:, 1:2], func=AF.Exp),
             reads=[("hv",)], writes=[("hv",)])
        p.op("dve", lambda e: e.tensor_scalar_mul(hv[:, 2:3], hv[:, 2:3], -1.0),
             reads=[("hv",)], writes=[("hv",)])

        gam = [lnwb[:, c:c + 1] for c in range(8)]
        bet = [lnwb[:, 8 + c:9 + c] for c in range(8)]

        def ln_in(tt):
            for c in range(NC):
                p.dma("sp", lambda e, c=c, tt=tt: e.dma_start(
                    out=xt[:, c * TT:(c + 1) * TT],
                    in_=x_h[c * 128:(c + 1) * 128, tt * TT:(tt + 1) * TT]),
                    writes=[("xt", c)], sem=f"dx{c}")
            if first:
                ln_feat(p, env, [xt[:, c * TT:(c + 1) * TT] for c in range(NC)],
                        [("xt", c) for c in range(NC)], cast=True,
                        dst_aps=[h0b[:, c * TT:(c + 1) * TT] for c in range(NC)],
                        dst_keys=[("h0b", c) for c in range(NC)], uid=f"ltop{tt}")
                ln_feat(p, env, [h0b[:, c * TT:(c + 1) * TT] for c in range(NC)],
                        [("h0b", c) for c in range(NC)], cast=True,
                        gamma=gam, beta=bet, gkeys=[("lnwb",)], uid=f"lgb{tt}")
            else:
                ln_feat(p, env, [xt[:, c * TT:(c + 1) * TT] for c in range(NC)],
                        [("xt", c) for c in range(NC)], cast=True,
                        gamma=gam, beta=bet, gkeys=[("lnwb",)], uid=f"lgb{tt}")
            return ([env.tmp_xb[:, c * TT:(c + 1) * TT] for c in range(NC)],
                    [("xb", c) for c in range(NC)])

        def halo_prep():
            for c in range(NC):
                p.dma("pool", lambda e, c=c: e.dma_start(
                    out=halo3[:, c * 3:(c + 1) * 3],
                    in_=xhalo_h[c * 128:(c + 1) * 128, 1:4]),
                    writes=[("halo3",)], sem="dh3")
            for t in range(3):
                colv = halo3[:, [c * 3 + t for c in range(NC)]] if False else None
                tok = halo3[:].rearrange("p (c t) -> p t c", t=3)[:, t, :]
                if first:
                    col_ln(p, env, tok, ("halo3",))
                col_ln_gb(p, env, tok, ("halo3",), lnwb)
            p.op("dve", lambda e: e.tensor_copy(scv[:, 0:24], halo3[:]),
                 reads=[("halo3",)], writes=[("scv",)])
            for m in range(16, 33):
                slot = wsl[m % 2]
                p.dma("pool", lambda e, m=m, slot=slot: e.dma_start(
                    out=slot[:].rearrange("p (k j) -> p k j", j=128), in_=inw_h[m]),
                    writes=[("wsl", m % 2)], sem=f"dwi{m % 2}")
                bank = env.psum()
                for j in range(NC):
                    p.op("pe", lambda e, m=m, j=j, slot=slot, bank=bank: e.matmul(
                        bank[:, 0:3], slot[:, j * 128:(j + 1) * 128],
                        scv[:, j * 3:(j + 1) * 3],
                        start=(j == 0), stop=(j == NC - 1)),
                        reads=[("wsl", m % 2), ("scv",)], writes=[K(bank)])
                c17 = m - 16
                p.op("dve", lambda e, c17=c17, bank=bank: e.tensor_scalar(
                    xbc[:, c17 * MB:c17 * MB + 3], bank[:, 0:3], hmask[:, 0:1],
                    None, ALU.mult),
                    reads=[K(bank), ("hmask",)], writes=[("xbc", c17)])

        def inproj(tt, uaps, ukeys):
            for m in range(34):
                slot = wsl[m % 2]
                p.dma("pool", lambda e, m=m, slot=slot: e.dma_start(
                    out=slot[:].rearrange("p (k j) -> p k j", j=128), in_=inw_h[m]),
                    writes=[("wsl", m % 2)], sem=f"dwi{m % 2}")
                bank = env.psum()
                for j in range(NC):
                    p.op("pe", lambda e, m=m, j=j, slot=slot, bank=bank, uaps=uaps: e.matmul(
                        bank[:, 0:TT], slot[:, j * 128:(j + 1) * 128], uaps[j],
                        start=(j == 0), stop=(j == NC - 1)),
                        reads=[("wsl", m % 2), ukeys[j]], writes=[K(bank)])
                if m < 16:
                    p.op("dve", lambda e, m=m, bank=bank: e.tensor_copy(
                        zb[:, m * TT:(m + 1) * TT], bank[:, 0:TT]),
                        reads=[K(bank)], writes=[("zb", m)])
                elif m < 33:
                    c17 = m - 16
                    p.op("dve", lambda e, c17=c17, bank=bank: e.tensor_copy(
                        xbc[:, c17 * MB + 3:c17 * MB + MB], bank[:, 0:TT]),
                        reads=[K(bank)], writes=[("xbc", c17)])
                else:
                    p.op("dve", lambda e, bank=bank: e.tensor_copy(
                        dtr[:], bank[0:32, 0:TT]),
                        reads=[K(bank)], writes=[("dtr",)])

        def conv_silu(tt):
            for c17 in range(17):
                base = c17 * MB
                w4 = [convw[:, c17 * 4 + d:c17 * 4 + d + 1] for d in range(4)]
                p.op("dve", lambda e, base=base, w4=w4: e.tensor_scalar(
                    scv[:, 0:TT], xbc[:, base:base + TT], w4[0], None, ALU.mult),
                    reads=[("xbc", c17), ("cw",)], writes=[("scv",)])
                for d in range(1, 4):
                    p.op("dve", lambda e, base=base, d=d, w4=w4: e.scalar_tensor_tensor(
                        scv[:, 0:TT], xbc[:, base + d:base + d + TT], w4[d],
                        scv[:, 0:TT], ALU.mult, ALU.add),
                        reads=[("xbc", c17), ("cw",), ("scv",)], writes=[("scv",)])
                if c17 < 16:
                    p.op("act", lambda e, c17=c17: e.activation(
                        out=xh[:, c17 * TT:(c17 + 1) * TT], in_=scv[:, 0:TT],
                        func=AF.Silu, bias=convb[:, c17:c17 + 1]),
                        reads=[("scv",), ("cb",)], writes=[("xh", c17)])
                else:
                    p.op("act", lambda e: e.activation(
                        out=scv[:, 0:TT], in_=scv[:, 0:TT],
                        func=AF.Silu, bias=convb[:, 16:17]),
                        reads=[("scv",), ("cb",)], writes=[("scv",)])
                    p.op("dve", lambda e: e.tensor_copy(env.bcq[:], scv[:, 0:TT]),
                         reads=[("scv",)], writes=[("bcq",)])
                p.op("dve", lambda e, c17=c17, base=base: e.tensor_copy(
                    xbc[:, base:base + 3], xbc[:, base + TT:base + TT + 3]),
                    reads=[("xbc", c17)], writes=[("xbc", c17)])

        env.bcq = sbuf(nc, es, "bcq", (128, TT), BF16)

        def dt_math(tt):
            p.op("dve", lambda e: e.tensor_scalar(dtr[:], dtr[:], hv[:, 0:1], None,
                                                  ALU.add),
                 reads=[("dtr",), ("hv",)], writes=[("dtr",)])
            p.op("act", lambda e: e.activation(out=grow[:], in_=dtr[:], func=AF.Abs),
                 reads=[("dtr",)], writes=[("grow",)])
            p.op("act", lambda e: e.activation(out=grow[:], in_=grow[:], func=AF.Exp,
                                               scale=-1.0),
                 reads=[("grow",)], writes=[("grow",)])
            one = env.const(1.0)
            p.op("act", lambda e, one=one: e.activation(out=grow[:], in_=grow[:],
                                                        func=AF.Ln, bias=one[0:32, :]),
                 reads=[("grow",), env.const_key(1.0)], writes=[("grow",)])
            p.op("dve", lambda e: e.tensor_scalar_max(wS[:], dtr[:], 0.0),
                 reads=[("dtr",)], writes=[("wS",)])
            p.op("dve", lambda e: e.tensor_add(dtr[:], wS[:], grow[:]),
                 reads=[("wS",), ("grow",)], writes=[("dtr",)])
            p.op("dve", lambda e: e.tensor_scalar(grow[:], dtr[:], hv[:, 2:3], None,
                                                  ALU.mult),
                 reads=[("dtr",), ("hv",)], writes=[("grow",)])
            for ch in range(NCH):
                p.op("dve", lambda e, ch=ch: e.tensor_tensor_scan(
                    cumA[:, ch * LC:(ch + 1) * LC], env.ones512f[0:32, 0:LC],
                    grow[:, ch * LC:(ch + 1) * LC], 0.0, ALU.mult, ALU.add),
                    reads=[("grow",), ("ones512f",)], writes=[("cumA",)])
            p.op("act", lambda e: e.activation(out=grow[:], in_=dtr[:], func=AF.Ln),
                 reads=[("dtr",)], writes=[("grow",)])
            p.op("dve", lambda e: e.tensor_sub(grow[:], grow[:], cumA[:]),
                 reads=[("grow",), ("cumA",)], writes=[("grow",)])
            p.op("act", lambda e: e.activation(out=ecA[:], in_=cumA[:], func=AF.Exp),
                 reads=[("cumA",)], writes=[("ecA",)])
            for ch in range(NCH):
                end = ch * LC + LC - 1
                p.op("dve", lambda e, ch=ch, end=end: e.tensor_scalar(
                    wS[:, ch * LC:(ch + 1) * LC], grow[:, ch * LC:(ch + 1) * LC],
                    cumA[:, end:end + 1], None, ALU.add),
                    reads=[("grow",), ("cumA",)], writes=[("wS",)])
            p.op("act", lambda e: e.activation(out=wS[:], in_=wS[:], func=AF.Exp),
                 reads=[("wS",)], writes=[("wS",)])
            cb0 = env.psum()
            p.op("pe", lambda e, cb0=cb0: e.matmul(cb0[0:64, 0:TT], shf[:], env.bcq[:],
                                                   start=True, stop=True),
                 reads=[("shf",), ("bcq",)], writes=[K(cb0)])
            p.op("dve", lambda e, cb0=cb0: e.tensor_copy(c0b[:], cb0[0:64, 0:TT]),
                 reads=[K(cb0)], writes=[("c0b",)])

        env.ones512f = sbuf(nc, es, "ones512f", (128, TT), F32)
        p.op("dve", lambda e: e.memset(env.ones512f[:], 1.0), writes=[("ones512f",)])

        def scan_chunk(tt, ch):
            csl = slice(ch * LC, (ch + 1) * LC)
            p.op("dve", lambda e: e.tensor_copy(Sb[:], S[:]),
                 reads=[("S",)], writes=[("Sb",)])
            cb_ = env.psum()
            p.op("pe", lambda e, csl=csl, cb_=cb_: e.matmul(
                cb_[:, 0:LC], env.bcq[0:64, csl], c0b[:, csl],
                start=True, stop=True),
                reads=[("bcq",), ("c0b",)], writes=[K(cb_)])
            p.op("dve", lambda e, cb_=cb_: e.tensor_mul(cbt[:], cb_[:, 0:LC], maskt[:]),
                 reads=[K(cb_), ("maskt",)], writes=[("cbt",)])
            for cc in range(16):
                p.op("pe", lambda e, cc=cc, csl=csl: e.transpose(
                    env.ps_bf[:, 0:128], xh[:, cc * TT + ch * LC:cc * TT + (ch + 1) * LC],
                    env.identb[:]),
                    reads=[("xh", cc), ("identb",)], writes=[K(env.ps_bf)])
                p.op("dve", lambda e, cc=cc: e.tensor_copy(
                    xhT[:, cc * 128:(cc + 1) * 128], env.ps_bf[:, 0:128]),
                    reads=[K(env.ps_bf)], writes=[("xhT", cc)])
            p.op("pe", lambda e, csl=csl: e.transpose(
                env.ps_bf[:, 0:64], env.bcq[0:64, csl], env.identb[0:64, 0:64]),
                reads=[("bcq",), ("identb",)], writes=[K(env.ps_bf)])
            p.op("dve", lambda e: e.tensor_copy(bt[:], env.ps_bf[:, 0:64]),
                 reads=[K(env.ps_bf)], writes=[("bt",)])
            for g4 in range(8):
                bank = env.psum()
                for hh in range(4):
                    h = g4 * 4 + hh
                    p.op("pe", lambda e, h=h, hh=hh, csl=csl, bank=bank: e.matmul(
                        bank[:, hh * 128:(hh + 1) * 128],
                        ehf[:, h * 128:(h + 1) * 128], cumA[:, csl],
                        start=True, stop=False),
                        reads=[("ehf",), ("cumA",)], writes=[K(bank)])
                    p.op("pe", lambda e, h=h, hh=hh, csl=csl, bank=bank: e.matmul(
                        bank[:, hh * 128:(hh + 1) * 128],
                        grow[:, csl], ehf[:, h * 128:(h + 1) * 128],
                        start=False, stop=True),
                        reads=[("ehf",), ("grow",)], writes=[K(bank)])
                p.op("dve", lambda e, bank=bank: e.tensor_scalar_min(
                    bank[:, 0:512], bank[:, 0:512], 30.0),
                    reads=[K(bank)], writes=[K(bank)])
                p.op("act", lambda e, bank=bank: e.activation(out=et4[:], in_=bank[:, 0:512],
                                                              func=AF.Exp),
                     reads=[K(bank)], writes=[("et4",)])
                for hh in range(4):
                    p.op("dve", lambda e, hh=hh: e.tensor_mul(
                        mpt[:, hh * 128:(hh + 1) * 128], et4[:, hh * 128:(hh + 1) * 128],
                        cbt[:]),
                        reads=[("et4",), ("cbt",)], writes=[("mpt", hh)])
                for pr in range(2):
                    pair = g4 * 2 + pr
                    ybank = env.psum()
                    for hh2 in range(2):
                        h = g4 * 4 + pr * 2 + hh2
                        hloc = pr * 2 + hh2
                        hr = 64 * hh2
                        cd = env.psum()
                        p.op("pe", lambda e, h=h, csl=csl, cd=cd: e.matmul(
                            cd[0:64, 0:LC], ehf[:, h * 128:h * 128 + 64], ecA[:, csl],
                            start=True, stop=True),
                            reads=[("ehf",), ("ecA",)], writes=[K(cd)])
                        p.op("dve", lambda e, csl=csl, cd=cd: e.tensor_mul(
                            cdec[:], cd[0:64, 0:LC], c0b[:, csl]),
                            reads=[K(cd), ("c0b",)], writes=[("cdec",)])
                        p.op("pe", lambda e, h=h, hr=hr, ybank=ybank: e.matmul(
                            ybank[hr:hr + 64, 0:LC], Sb[:, h * 64:(h + 1) * 64],
                            cdec[:], start=True, stop=False),
                            reads=[("Sb",), ("cdec",)], writes=[K(ybank)])
                        cc2 = h // 2
                        co = 64 * (h % 2)
                        p.op("pe", lambda e, cc2=cc2, co=co, hr=hr, hloc=hloc,
                             ybank=ybank: e.matmul(
                            ybank[hr:hr + 64, 0:LC],
                            xhT[:, cc2 * 128 + co:cc2 * 128 + co + 64],
                            mpt[:, hloc * 128:(hloc + 1) * 128],
                            start=False, stop=True),
                            reads=[("xhT", cc2), ("mpt", hloc)],
                            writes=[K(ybank)])
                    cc = pair
                    p.op("dve", lambda e, cc=cc, ch=ch, ybank=ybank: e.scalar_tensor_tensor(
                        yb[:, cc * TT + ch * LC:cc * TT + (ch + 1) * LC],
                        xh[:, cc * TT + ch * LC:cc * TT + (ch + 1) * LC],
                        drms[:, cc:cc + 1], ybank[:, 0:LC], ALU.mult, ALU.add),
                        reads=[("xh", cc), ("drms",), K(ybank)], writes=[("yb", cc)])
            for pair in range(16):
                for hh2 in range(2):
                    h = pair * 2 + hh2
                    wb = env.psum()
                    p.op("pe", lambda e, h=h, csl=csl, wb=wb: e.matmul(
                        wb[:, 0:64], wS[:, csl], ehf[:, h * 128:h * 128 + 64],
                        start=True, stop=True),
                        reads=[("wS",), ("ehf",)], writes=[K(wb)])
                    p.op("dve", lambda e, h=h, hh2=hh2, pair=pair, wb=wb: e.tensor_mul(
                        xwt[:, hh2 * 64:(hh2 + 1) * 64], wb[:, 0:64],
                        xhT[:, pair * 128 + hh2 * 64:pair * 128 + hh2 * 64 + 64]),
                        reads=[K(wb), ("xhT", pair)], writes=[("xwt",)])
                ds = env.psum()
                p.op("pe", lambda e, ds=ds: e.matmul(ds[0:64, 0:128], bt[:, 0:64], xwt[:],
                                                     start=True, stop=True),
                     reads=[("bt",), ("xwt",)], writes=[K(ds)])
                dc = env.psum()
                end = ch * LC + LC - 1
                p.op("dve", lambda e, pair=pair, end=end: e.tensor_scalar(
                    h2s[:], h2c[:, pair * 128:(pair + 1) * 128], ecA[:, end:end + 1],
                    None, ALU.mult),
                    reads=[("h2c",), ("ecA",)], writes=[("h2s",)])
                p.op("pe", lambda e, dc=dc: e.matmul(dc[0:64, 0:128], env.ones32f[:], h2s[:],
                                                     start=True, stop=True),
                     reads=[("h2s",), ("ones32f",)], writes=[K(dc)])
                p.op("dve", lambda e, pair=pair, dc=dc: e.tensor_mul(
                    S[:, pair * 128:(pair + 1) * 128], S[:, pair * 128:(pair + 1) * 128],
                    dc[0:64, 0:128]),
                    reads=[("S",), K(dc)], writes=[("S",)])
                p.op("dve", lambda e, pair=pair, ds=ds: e.tensor_add(
                    S[:, pair * 128:(pair + 1) * 128], S[:, pair * 128:(pair + 1) * 128],
                    ds[0:64, 0:128]),
                    reads=[("S",), K(ds)], writes=[("S",)])

        env.ones32f = sbuf(nc, es, "ones32f", (32, 64), F32)
        p.op("dve", lambda e: e.memset(env.ones32f[:], 1.0), writes=[("ones32f",)])

        def finish_tile(tt):
            for cc in range(16):
                p.op("act", lambda e, cc=cc: e.activation(out=scv[:, 0:TT],
                                                          in_=zb[:, cc * TT:(cc + 1) * TT],
                                                          func=AF.Silu),
                     reads=[("zb", cc)], writes=[("scv",)])
                p.op("dve", lambda e, cc=cc: e.tensor_mul(
                    yb[:, cc * TT:(cc + 1) * TT], yb[:, cc * TT:(cc + 1) * TT],
                    scv[:, 0:TT]),
                    reads=[("yb", cc), ("scv",)], writes=[("yb", cc)])
            stat = env.ps_stat
            for cc in range(16):
                p.op("dve", lambda e, cc=cc: e.tensor_mul(
                    env.tmp_x2b[:], yb[:, cc * TT:(cc + 1) * TT],
                    yb[:, cc * TT:(cc + 1) * TT]),
                    reads=[("yb", cc)], writes=[("x2b",)])
                p.op("pe", lambda e, cc=cc: e.matmul(stat[0:1, 0:TT], env.ones_col[:],
                                                     env.tmp_x2b[:],
                                                     start=(cc == 0), stop=(cc == 15)),
                     reads=[("x2b",), ("ones_col",)], writes=[K(stat)])
            rows = env.tmp_rows
            rA = rows[0:1, 0:TT]
            p.op("dve", lambda e: e.tensor_scalar_mul(rA, stat[0:1, 0:TT], 1.0 / 2048.0),
                 reads=[K(stat)], writes=[("rA",)])
            epsap = env.const(1e-5)
            p.op("act", lambda e, epsap=epsap: e.activation(out=rA, in_=rA, func=AF.Sqrt,
                                                            bias=epsap[0:1, :]),
                 reads=[("rA",), env.const_key(1e-5)], writes=[("rA",)])
            p.op("dve", lambda e: e.reciprocal(rA, rA), reads=[("rA",)], writes=[("rA",)])
            bc = env.psum()
            p.op("pe", lambda e, bc=bc: e.matmul(bc[:, 0:TT], env.ones1f[:], rA,
                                                 start=True, stop=True),
                 reads=[("rA",), ("ones1f",)], writes=[K(bc)])
            p.op("dve", lambda e, bc=bc: e.tensor_copy(env.tmp_mb[:], bc[:, 0:TT]),
                 reads=[K(bc)], writes=[("mb",)])
            for cc in range(16):
                p.op("dve", lambda e, cc=cc: e.tensor_mul(
                    yb[:, cc * TT:(cc + 1) * TT], yb[:, cc * TT:(cc + 1) * TT],
                    env.tmp_mb[:]),
                    reads=[("yb", cc), ("mb",)], writes=[("yb", cc)])
                p.op("act", lambda e, cc=cc: e.activation(
                    out=yb[:, cc * TT:(cc + 1) * TT], in_=yb[:, cc * TT:(cc + 1) * TT],
                    func=AF.Identity, scale=drms[:, 16 + cc:17 + cc]),
                    reads=[("yb", cc), ("drms",)], writes=[("yb", cc)])
            def epo(m, bank):
                if first:
                    p.op("dve", lambda e, m=m, tt=tt, bank=bank: e.tensor_add(
                        xt[:, m * TT:(m + 1) * TT], bank[:, 0:TT],
                        h0b[:, m * TT:(m + 1) * TT]),
                        reads=[K(bank), ("h0b", m)], writes=[("xt", m)])
                else:
                    p.op("dve", lambda e, m=m, tt=tt, bank=bank: e.tensor_add(
                        xt[:, m * TT:(m + 1) * TT], bank[:, 0:TT],
                        xt[:, m * TT:(m + 1) * TT]),
                        reads=[K(bank), ("xt", m)], writes=[("xt", m)])
                    p.dma("pool", lambda e, m=m, tt=tt: e.dma_start(
                        out=xrt[:], in_=xres_h[m * 128:(m + 1) * 128,
                                               tt * TT:(tt + 1) * TT]),
                        writes=[("xrt",)], sem="dxr")
                    p.op("dve", lambda e, m=m, bank=bank: e.tensor_add(
                        xt[:, m * TT:(m + 1) * TT], xt[:, m * TT:(m + 1) * TT],
                        xrt[:]),
                        reads=[("xt", m), ("xrt",)], writes=[("xt", m)])
                p.dma("sp", lambda e, m=m, tt=tt: e.dma_start(
                    out=y_h[m * 128:(m + 1) * 128, tt * TT:(tt + 1) * TT],
                    in_=xt[:, m * TT:(m + 1) * TT]),
                    reads=[("xt", m)], sem="dy")
            stream_gemm(p, env, wpre_h=outw_h, nk=16, nm=8,
                        rhs_ap=lambda k: yb[:, k * TT:(k + 1) * TT],
                        rhs_keys=lambda k: [("yb", k)],
                        out_fn=epo, slots=wosl, slot_base="wosl", sem="dwo",
                        tag=f"op{tt}")

        halo_prep()
        for tt in range(NTT):
            uaps, ukeys = ln_in(tt)
            inproj(tt, uaps, ukeys)
            conv_silu(tt)
            dt_math(tt)
            for ch in range(NCH):
                scan_chunk(tt, ch)
            finish_tile(tt)
        p.dma("sp", lambda e: e.dma_start(out=st_h[:, :], in_=S[:]),
              reads=[("S",)], sem="dy")
        if shared is None:
            p.finalize(es)
    return nc


def col_ln_gb(p, env, col, ckey, lnwb):
    col_ln(p, env, col, ckey)
    for c in range(NC):
        p.op("act", lambda e, c=c: e.activation(
            out=col[:, c:c + 1], in_=col[:, c:c + 1], func=AF.Identity,
            scale=lnwb[:, c:c + 1], bias=lnwb[:, 8 + c:9 + c]),
            reads=[ckey, ("lnwb",)], writes=[ckey])





def build_delta(shared=None, x_ap=None, y_ap=None, pfx=""):
    if shared is None:
        nc = bass.Bass("TRN2", target_bir_lowering=False)
        p0 = Prog(nc)
    else:
        nc, p0 = shared

    def din(name, shape):
        return nc.dram_tensor(pfx + name, list(shape), F32, kind="ExternalInput").ap()

    x_h = x_ap if x_ap is not None else din("x", [D, TCORE])
    xhalo_h = din("xhalo", [D, 4])
    wq_h = din("wq", [8, 128, 8, 128])
    wk_h = din("wk", [8, 128, 8, 128])
    wv_h = din("wv", [8, 128, 8, 128])
    wg_h = din("wg", [128, 8, 1024])
    wo_h = din("wo", [8, 128, 8, 128])
    wab_h = din("wab", [2, 128, 8, 128])
    cqkv_h = din("cqkv", [128, 96])
    hv_h = din("hv", [16, 2])
    nw_h = din("nw", [128, 64])
    sin_h = din("sin", [64, 1024])
    ehf_h = din("ehf", [16, 16 * 128])
    h2_h = din("h2c", [16, 1024])
    mneg_h = din("mneg", [128, 128])
    mpos_h = din("mpos", [128, 128])
    hmask_h = din("hmask", [128, 1])
    e2ad_h = din("e2ad", [128, 2])
    e2bd_h = din("e2bd", [2, 128])
    y_h = y_ap if y_ap is not None else nc.dram_tensor(
        pfx + "y", [D, TCORE], F32, kind="ExternalOutput").ap()
    st_h = nc.dram_tensor(pfx + "st", [64, 1024], F32, kind="ExternalOutput").ap()

    LC = 128
    NCH = 4
    MB = TT + 3
    with contextlib.ExitStack() as es:
        p = p0
        env = Env(nc, p, es)
        env.const(1e-5)
        env.const(1.0)
        alloc_ln_tmps(nc, env, es)
        make_ident(p, env, es)
        xt = sbuf(nc, es, "xt", (128, NC * TT), F32)
        ub = sbuf(nc, es, "ub", (128, NC * TT), BF16)
        qm = sbuf(nc, es, "qm", (128, 8 * MB), BF16)
        km = sbuf(nc, es, "km", (128, 8 * MB), BF16)
        vm = sbuf(nc, es, "vm", (128, 8 * MB), BF16)
        gr = sbuf(nc, es, "gr", (16, TT), F32)
        br = sbuf(nc, es, "br", (16, TT), F32)
        cumG = sbuf(nc, es, "cumG", (16, TT), F32)
        ecG = sbuf(nc, es, "ecG", (16, TT), F32)
        hv = sbuf(nc, es, "hvs", (16, 4), F32)
        S = sbuf(nc, es, "S", (64, 1024), F32)
        cqkv = sbuf(nc, es, "cqkvs", (128, 96), F32)

        ehf = sbuf(nc, es, "ehfs", (16, 16 * 128), F32)
        h2c = sbuf(nc, es, "h2cs", (16, 1024), F32)
        h2s = sbuf(nc, es, "h2ss", (16, 1024), F32)
        mneg = sbuf(nc, es, "mnegs", (128, 128), F32)
        mpos = sbuf(nc, es, "mposs", (128, 128), F32)
        hmask = sbuf(nc, es, "hmasks", (128, 1), F32)
        etile = sbuf(nc, es, "etile", (128, 128), BF16)
        ltile = sbuf(nc, es, "ltile", (128, 128), BF16)

        ntp = sbuf(nc, es, "ntp", (128, 128), BF16)
        xtile = sbuf(nc, es, "xtile", (128, 128), BF16)
        kTt = sbuf(nc, es, "kTt", (128, 128), BF16)
        vTt = sbuf(nc, es, "vTt", (128, 128), BF16)
        uU = sbuf(nc, es, "uU", (128, 64), BF16)
        kw2 = sbuf(nc, es, "kw2", (128, 64), BF16)
        colw = sbuf(nc, es, "colw", (128, 4), F32)
        otm = sbuf(nc, es, "otm", (128, 4 * 1024), BF16)
        gtm = sbuf(nc, es, "gtm", (128, 1024), BF16)
        ofm = sbuf(nc, es, "ofm", (128, 8 * TT), BF16)
        wgs = sbuf(nc, es, "wgs", (128, 8 * 1024), BF16)
        scv = sbuf(nc, es, "scv", (128, TT), BF16)
        halo3 = sbuf(nc, es, "halo3", (128, 24), F32)
        wsl = [sbuf(nc, es, f"wsl{i}", (128, 8 * 128), BF16) for i in range(2)]
        Sb = sbuf(nc, es, "Sb", (64, 1024), BF16)
        qfm = sbuf(nc, es, "qfm", (64, 128), BF16)

        for nm, dst, hh in (("cqkv", cqkv, cqkv_h),
                            ("ehf", ehf, ehf_h), ("h2c", h2c, h2_h),
                            ("mneg", mneg, mneg_h), ("mpos", mpos, mpos_h),
                            ("hmask", hmask, hmask_h)):
            p.dma("pool", lambda e, dst=dst, hh=hh: e.dma_start(out=dst[:], in_=hh[:, :]),
                  writes=[(nm,)], sem="dc" + nm)
        p.dma("pool", lambda e: e.dma_start(
            out=wgs[:].rearrange("p (c j) -> p c j", j=1024), in_=wg_h[:, :, :]),
            writes=[("wgs",)], sem="dcwg")
        p.dma("sp", lambda e: e.dma_start(out=hv[:, 0:2], in_=hv_h[:, :]),
              writes=[("hv",)], sem="dchv")
        p.dma("sp", lambda e: e.dma_start(out=S[:], in_=sin_h[:, :]),
              writes=[("S",)], sem="dcS")
        p.op("act", lambda e: e.activation(out=hv[:, 2:3], in_=hv[:, 1:2], func=AF.Exp),
             reads=[("hv",)], writes=[("hv",)])
        p.op("dve", lambda e: e.tensor_scalar_mul(hv[:, 2:3], hv[:, 2:3], -1.0),
             reads=[("hv",)], writes=[("hv",)])

        def proj_conv(tt):
            for c in range(NC):
                p.dma("sp", lambda e, c=c, tt=tt: e.dma_start(
                    out=xt[:, c * TT:(c + 1) * TT],
                    in_=x_h[c * 128:(c + 1) * 128, tt * TT:(tt + 1) * TT]),
                    writes=[("xt", c)], sem=f"dx{c}")
            ln_feat(p, env, [xt[:, c * TT:(c + 1) * TT] for c in range(NC)],
                    [("xt", c) for c in range(NC)], cast=True,
                    dst_aps=[ub[:, c * TT:(c + 1) * TT] for c in range(NC)],
                    dst_keys=[("ub", c) for c in range(NC)], uid=f"lt{tt}")
            for wname, wh, dst, dkey, coff in (("q", wq_h, qm, "qm", 0),
                                               ("k", wk_h, km, "km", 32),
                                               ("v", wv_h, vm, "vm", 64)):
                for m in range(NC):
                    slot = wsl[m % 2]
                    p.dma("pool", lambda e, m=m, wh=wh, slot=slot: e.dma_start(
                        out=slot[:].rearrange("p (k j) -> p k j", j=128), in_=wh[m]),
                        writes=[("wsl", m % 2)], sem=f"dw{m % 2}")
                    bank = env.psum()
                    for j in range(NC):
                        p.op("pe", lambda e, m=m, j=j, slot=slot, bank=bank: e.matmul(
                            bank[:, 0:TT], slot[:, j * 128:(j + 1) * 128],
                            ub[:, j * TT:(j + 1) * TT],
                            start=(j == 0), stop=(j == NC - 1)),
                            reads=[("wsl", m % 2), ("ub", j)], writes=[K(bank)])
                    p.op("dve", lambda e, m=m, dst=dst, bank=bank: e.tensor_copy(
                        dst[:, m * MB + 3:(m + 1) * MB], bank[:, 0:TT]),
                        reads=[K(bank)], writes=[(dkey, m)])
                for m in range(NC):
                    base = m * MB
                    dk = (dkey, m)
                    co = coff + m * 4
                    p.op("dve", lambda e, base=base, dst=dst, co=co: e.tensor_scalar(
                        scv[:, 0:TT], dst[:, base:base + TT],
                        cqkv[:, co:co + 1], None, ALU.mult),
                        reads=[dk, ("cqkv",)], writes=[("scv",)])
                    for d_ in range(1, 4):
                        p.op("dve", lambda e, d_=d_, base=base, dst=dst, co=co:
                             e.scalar_tensor_tensor(
                                 scv[:, 0:TT], dst[:, base + d_:base + d_ + TT],
                                 cqkv[:, co + d_:co + d_ + 1],
                                 scv[:, 0:TT], ALU.mult, ALU.add),
                             reads=[dk, ("cqkv",), ("scv",)], writes=[("scv",)])
                    p.op("dve", lambda e, base=base, dst=dst: e.tensor_copy(
                        dst[:, base:base + 3], dst[:, base + TT:base + TT + 3]),
                        reads=[dk], writes=[dk])
                    p.op("act", lambda e, base=base, dst=dst: e.activation(
                        out=dst[:, base + 3:base + 3 + TT], in_=scv[:, 0:TT], func=AF.Silu),
                        reads=[("scv",)], writes=[dk])

        def l2n(dst, dkey, scale):
            for m in range(NC):
                ap = dst[:, m * MB + 3:m * MB + 3 + TT]
                p.op("dve", lambda e, ap=ap: e.tensor_mul(env.tmp_x2b[:], ap, ap),
                     reads=[(dkey, m)], writes=[("x2b",)])
                nb = env.psum()
                p.op("pe", lambda e, nb=nb: e.matmul(nb[0:2, 0:TT], env.e2ad[:],
                                                     env.tmp_x2b[:],
                                                     start=True, stop=True),
                     reads=[("x2b",), ("e2ad",)], writes=[K(nb)])
                p.op("dve", lambda e, nb=nb: e.tensor_scalar_add(env.nrm2d[:, 0:TT],
                                                                 nb[0:2, 0:TT], 1e-6),
                     reads=[K(nb)], writes=[("nrm2d",)])
                p.op("act", lambda e: e.activation(out=env.nrm2d[:, 0:TT],
                                                   in_=env.nrm2d[:, 0:TT], func=AF.Sqrt),
                     reads=[("nrm2d",)], writes=[("nrm2d",)])
                p.op("dve", lambda e: e.reciprocal(env.nrm2d[:, 0:TT], env.nrm2d[:, 0:TT]),
                     reads=[("nrm2d",)], writes=[("nrm2d",)])
                nb2 = env.psum()
                p.op("pe", lambda e, nb2=nb2: e.matmul(nb2[:, 0:TT], env.e2bd[:],
                                                       env.nrm2d[:, 0:TT],
                                                       start=True, stop=True),
                     reads=[("nrm2d",), ("e2bd",)], writes=[K(nb2)])
                p.op("dve", lambda e, ap=ap, nb2=nb2: e.tensor_mul(ap, ap, nb2[:, 0:TT]),
                     reads=[(dkey, m), K(nb2)], writes=[(dkey, m)])
                if scale != 1.0:
                    p.op("dve", lambda e, ap=ap, scale=scale: e.tensor_scalar_mul(
                        ap, ap, scale),
                        reads=[(dkey, m)], writes=[(dkey, m)])

        def ab_rows(tt):
            for idx, (dstr, key) in enumerate((("b", "br"), ("a", "gr"))):
                slot = wsl[idx % 2]
                p.dma("pool", lambda e, idx=idx, slot=slot: e.dma_start(
                    out=slot[:].rearrange("p (k j) -> p k j", j=128), in_=wab_h[idx]),
                    writes=[("wsl", idx % 2)], sem=f"dw{idx % 2}")
                bank = env.psum()
                for j in range(NC):
                    p.op("pe", lambda e, idx=idx, j=j, slot=slot, bank=bank: e.matmul(
                        bank[0:16, 0:TT], slot[:, j * 128:j * 128 + 16],
                        ub[:, j * TT:(j + 1) * TT],
                        start=(j == 0), stop=(j == NC - 1)),
                        reads=[("wsl", idx % 2), ("ub", j)], writes=[K(bank)])
                if idx == 0:
                    p.op("act", lambda e, bank=bank: e.activation(
                        out=br[:], in_=bank[0:16, 0:TT], func=AF.Sigmoid),
                        reads=[K(bank)], writes=[("br",)])
                else:
                    p.op("dve", lambda e, bank=bank: e.tensor_copy(gr[:], bank[0:16, 0:TT]),
                         reads=[K(bank)], writes=[("gr",)])
            p.op("dve", lambda e: e.tensor_scalar(gr[:], gr[:], hv[:, 0:1], None, ALU.add),
                 reads=[("gr",), ("hv",)], writes=[("gr",)])
            p.op("act", lambda e: e.activation(out=ecG[:], in_=gr[:], func=AF.Abs),
                 reads=[("gr",)], writes=[("ecG",)])
            p.op("act", lambda e: e.activation(out=ecG[:], in_=ecG[:], func=AF.Exp,
                                               scale=-1.0),
                 reads=[("ecG",)], writes=[("ecG",)])
            one = env.const(1.0)
            p.op("act", lambda e, one=one: e.activation(out=ecG[:], in_=ecG[:], func=AF.Ln,
                                                        bias=one[0:16, :]),
                 reads=[("ecG",), env.const_key(1.0)], writes=[("ecG",)])
            p.op("dve", lambda e: e.tensor_scalar_max(cumG[:], gr[:], 0.0),
                 reads=[("gr",)], writes=[("cumG",)])
            p.op("dve", lambda e: e.tensor_add(gr[:], cumG[:], ecG[:]),
                 reads=[("cumG",), ("ecG",)], writes=[("gr",)])
            p.op("dve", lambda e: e.tensor_scalar(gr[:], gr[:], hv[:, 2:3], None, ALU.mult),
                 reads=[("gr",), ("hv",)], writes=[("gr",)])
            for ch in range(NCH):
                p.op("dve", lambda e, ch=ch: e.tensor_tensor_scan(
                    cumG[:, ch * LC:(ch + 1) * LC], env.ones512f[0:16, 0:LC],
                    gr[:, ch * LC:(ch + 1) * LC], 0.0, ALU.mult, ALU.add),
                    reads=[("gr",), ("ones512f",)], writes=[("cumG",)])
            p.op("act", lambda e: e.activation(out=ecG[:], in_=cumG[:], func=AF.Exp),
                 reads=[("cumG",)], writes=[("ecG",)])

        env.ones512f = sbuf(nc, es, "ones512f", (128, TT), F32)
        p.op("dve", lambda e: e.memset(env.ones512f[:], 1.0), writes=[("ones512f",)])
        env.e2ad = sbuf(nc, es, "e2ads", (128, 2), BF16)
        env.e2bd = sbuf(nc, es, "e2bds", (2, 128), F32)
        env.nrm2d = sbuf(nc, es, "nrm2d", (2, TT), F32)
        p.dma("pool", lambda e: e.dma_start(out=env.e2ad[:], in_=e2ad_h[:, :]),
              writes=[("e2ad",)], sem="dce2a")
        p.dma("pool", lambda e: e.dma_start(out=env.e2bd[:], in_=e2bd_h[:, :]),
              writes=[("e2bd",)], sem="dce2b")

        def scan_chunk(tt, ch):
            p.op("dve", lambda e: e.tensor_copy(Sb[:], S[:]),
                 reads=[("S",)], writes=[("Sb",)])
            for h in range(16):
                hc, hr = h // 2, 64 * (h % 2)
                qs = qm[:, hc * MB + 3 + ch * LC:hc * MB + 3 + (ch + 1) * LC]
                ks = km[:, hc * MB + 3 + ch * LC:hc * MB + 3 + (ch + 1) * LC]
                vs = vm[:, hc * MB + 3 + ch * LC:hc * MB + 3 + (ch + 1) * LC]
                gsl = slice(ch * LC, (ch + 1) * LC)
                eb = env.psum()
                p.op("pe", lambda e, h=h, gsl=gsl, eb=eb: e.matmul(
                    eb[:, 0:LC], ehf[:, h * 128:(h + 1) * 128], cumG[:, gsl],
                    start=True, stop=False),
                    reads=[("ehf",), ("cumG",)], writes=[K(eb)])
                p.op("dve", lambda e, gsl=gsl: e.tensor_scalar_mul(
                    env.nrm2d[0:16, 0:LC] if False else gr[:, gsl], cumG[:, gsl], -1.0),
                    reads=[("cumG",)], writes=[("gr",)])
                p.op("pe", lambda e, h=h, gsl=gsl, eb=eb: e.matmul(
                    eb[:, 0:LC], gr[:, gsl], ehf[:, h * 128:(h + 1) * 128],
                    start=False, stop=True),
                    reads=[("gr",), ("ehf",)], writes=[K(eb)])
                p.op("dve", lambda e, eb=eb: e.tensor_scalar_min(eb[:, 0:LC], eb[:, 0:LC],
                                                                 30.0),
                     reads=[K(eb)], writes=[K(eb)])
                p.op("act", lambda e, eb=eb: e.activation(out=etile[:], in_=eb[:, 0:LC],
                                                          func=AF.Exp),
                     reads=[K(eb)], writes=[("etile",)])
                kk = env.psum()
                p.op("pe", lambda e, h=h, hr=hr, ks=ks, kk=kk: e.matmul(
                    kk[:, 0:LC], ks[hr:hr + 64, :], ks[hr:hr + 64, :],
                    start=True, stop=True),
                    reads=[("km", h // 2)], writes=[K(kk)])
                bb = env.psum()
                p.op("pe", lambda e, h=h, gsl=gsl, bb=bb: e.matmul(
                    bb[:, 0:LC], ehf[:, h * 128:(h + 1) * 128], br[:, gsl],
                    start=True, stop=True),
                    reads=[("ehf",), ("br",)], writes=[K(bb)])
                p.op("dve", lambda e, kk=kk: e.tensor_mul(ltile[:], etile[:], kk[:, 0:LC]),
                     reads=[("etile",), K(kk)], writes=[("ltile",)])
                p.op("dve", lambda e, bb=bb: e.tensor_mul(ltile[:], ltile[:], bb[:, 0:LC]),
                     reads=[("ltile",), K(bb)], writes=[("ltile",)])
                p.op("dve", lambda e: e.tensor_mul(ltile[:], ltile[:], mneg[:]),
                     reads=[("ltile",), ("mneg",)], writes=[("ltile",)])
                tp = env.ps_bf
                p.op("pe", lambda e, ks=ks: e.transpose(tp[:, 0:128], ks, env.identb[:]),
                     reads=[("km", hc), ("identb",)], writes=[K(tp)])
                p.op("dve", lambda e, hr=hr: e.tensor_copy(kTt[:, 0:64], tp[:, hr:hr + 64]),
                     reads=[K(tp)], writes=[("kTt",)])
                p.op("pe", lambda e, vs=vs: e.transpose(tp[:, 0:128], vs, env.identb[:]),
                     reads=[("vm", hc), ("identb",)], writes=[K(tp)])
                p.op("dve", lambda e, hr=hr: e.tensor_copy(vTt[:, 0:64], tp[:, hr:hr + 64]),
                     reads=[K(tp)], writes=[("vTt",)])
                cw = env.psum()
                p.op("pe", lambda e, h=h, gsl=gsl, cw=cw: e.matmul(
                    cw[:, 0:2], br[:, gsl], ehf[:, h * 128:h * 128 + 2],
                    start=True, stop=False),
                    reads=[("br",), ("ehf",)], writes=[K(cw)])
                p.op("pe", lambda e, h=h, gsl=gsl, cw=cw: e.matmul(
                    cw[:, 2:4], ecG[:, gsl], ehf[:, h * 128:h * 128 + 2],
                    start=False, stop=True),
                    reads=[("ecG",), ("ehf",)], writes=[K(cw)])
                p.op("dve", lambda e, cw=cw: e.tensor_copy(colw[:], cw[:, 0:4]),
                     reads=[K(cw)], writes=[("colw",)])
                p.op("dve", lambda e: e.tensor_scalar(xtile[:, 0:64], vTt[:, 0:64],
                                                      colw[:, 0:1], None, ALU.mult),
                     reads=[("vTt",), ("colw",)], writes=[("xtile",)])
                p.op("dve", lambda e: e.tensor_scalar(xtile[:, 64:128], kTt[:, 0:64],
                                                      colw[:, 0:1], None, ALU.mult),
                     reads=[("kTt",), ("colw",)], writes=[("xtile",)])
                p.op("dve", lambda e: e.tensor_scalar(xtile[:, 64:128], xtile[:, 64:128],
                                                      colw[:, 2:3], None, ALU.mult),
                     reads=[("xtile",), ("colw",)], writes=[("xtile",)])
                for kpow in range(7):
                    xp_ = env.psum()
                    p.op("pe", lambda e, xp_=xp_: e.matmul(xp_[:, 0:128], ltile[:],
                                                           xtile[:], start=True, stop=True),
                         reads=[("ltile",), ("xtile",)], writes=[K(xp_)])
                    p.op("dve", lambda e, xp_=xp_: e.tensor_add(xtile[:], xtile[:],
                                                                xp_[:, 0:128]),
                         reads=[("xtile",), K(xp_)], writes=[("xtile",)])
                    if kpow < 6:
                        p.op("pe", lambda e: e.transpose(tp[:, 0:128], ltile[:],
                                                         env.identb[:]),
                             reads=[("ltile",), ("identb",)], writes=[K(tp)])
                        p.op("dve", lambda e: e.tensor_copy(ntp[:], tp[:, 0:128]),
                             reads=[K(tp)], writes=[("ntp",)])
                        sq_ = env.psum()
                        p.op("pe", lambda e, sq_=sq_: e.matmul(sq_[:, 0:128], ntp[:],
                                                               ltile[:],
                                                               start=True, stop=True),
                             reads=[("ntp",), ("ltile",)], writes=[K(sq_)])
                        p.op("dve", lambda e, sq_=sq_: e.tensor_copy(ltile[:],
                                                                     sq_[:, 0:128]),
                             reads=[K(sq_)], writes=[("ltile",)])
                p.op("pe", lambda e: e.transpose(tp[0:64, 0:128], xtile[:, 64:128],
                                                 env.identb[:]),
                     reads=[("xtile",), ("identb",)], writes=[K(tp)])
                p.op("dve", lambda e: e.tensor_copy(ntp[0:64, 0:128], tp[0:64, 0:128]),
                     reads=[K(tp)], writes=[("ntp",)])
                us = env.psum()
                p.op("pe", lambda e, h=h, us=us: e.matmul(
                    us[:, 0:64], ntp[0:64, 0:128], Sb[:, h * 64:(h + 1) * 64],
                    start=True, stop=True),
                    reads=[("ntp",), ("Sb",)], writes=[K(us)])
                p.op("dve", lambda e, us=us: e.tensor_sub(uU[:, 0:64], xtile[:, 0:64],
                                                          us[:, 0:64]),
                     reads=[("xtile",), K(us)], writes=[("uU",)])
                p.op("pe", lambda e, qs=qs: e.transpose(tp[:, 0:128], qs, env.identb[:]),
                     reads=[("qm", hc), ("identb",)], writes=[K(tp)])
                p.op("dve", lambda e, hr=hr: e.tensor_scalar(
                    kw2[:, 0:64], tp[:, hr:hr + 64], colw[:, 2:3], None, ALU.mult),
                    reads=[K(tp), ("colw",)], writes=[("kw2",)])
                kq = env.psum()
                p.op("pe", lambda e, h=h, hr=hr, ks=ks, qs=qs, kq=kq: e.matmul(
                    kq[:, 0:LC], ks[hr:hr + 64, :], qs[hr:hr + 64, :],
                    start=True, stop=True),
                    reads=[("km", hc), ("qm", hc)], writes=[K(kq)])
                p.op("dve", lambda e, kq=kq: e.tensor_mul(ntp[:], etile[:], kq[:, 0:LC]),
                     reads=[("etile",), K(kq)], writes=[("ntp",)])
                p.op("dve", lambda e: e.tensor_mul(ntp[:], ntp[:], mpos[:]),
                     reads=[("ntp",), ("mpos",)], writes=[("ntp",)])
                ob = env.psum()
                p.op("pe", lambda e: e.transpose(tp[0:64, 0:128], kw2[:, 0:64],
                                                 env.identb[:]),
                     reads=[("kw2",), ("identb",)], writes=[K(tp)])
                p.op("dve", lambda e: e.tensor_copy(qfm[:], tp[0:64, 0:128]),
                     reads=[K(tp)], writes=[("qfm",)])
                p.op("pe", lambda e, h=h, ob=ob: e.matmul(
                    ob[:, 0:64], qfm[:], Sb[:, h * 64:(h + 1) * 64],
                    start=True, stop=False),
                    reads=[("qfm",), ("Sb",)], writes=[K(ob)])
                p.op("pe", lambda e, ob=ob: e.matmul(
                    ob[:, 0:64], ntp[:], uU[:, 0:64],
                    start=False, stop=True),
                    reads=[("ntp",), ("uU",)], writes=[K(ob)])
                p.op("dve", lambda e, h=h, ch=ch, ob=ob: e.tensor_copy(
                    otm[:, ch * 1024 + h * 64:ch * 1024 + (h + 1) * 64], ob[:, 0:64]),
                    reads=[K(ob)], writes=[("otm", ch, h)])
                p.op("dve", lambda e: e.tensor_copy(colw[:, 3:4], etile[:, 127:128]),
                     reads=[("etile",)], writes=[("colw",)])
                p.op("dve", lambda e: e.tensor_scalar(kw2[:, 0:64], kTt[:, 0:64],
                                                      colw[:, 3:4], None, ALU.mult),
                     reads=[("kTt",), ("colw",)], writes=[("kw2",)])
                ds = env.psum()
                p.op("pe", lambda e, ds=ds: e.matmul(ds[0:64, 0:64], kw2[:, 0:64],
                                                     uU[:, 0:64], start=True, stop=True),
                     reads=[("kw2",), ("uU",)], writes=[K(ds)])
                dc = env.psum()
                end = ch * LC + LC - 1
                p.op("dve", lambda e, h=h, end=end: e.tensor_scalar(
                    h2s[:, h * 64:(h + 1) * 64], h2c[:, h * 64:(h + 1) * 64],
                    ecG[:, end:end + 1], None, ALU.mult),
                    reads=[("h2c",), ("ecG",)], writes=[("h2s",)])
                p.op("pe", lambda e, h=h, dc=dc: e.matmul(
                    dc[0:64, 0:64], env.ones16f[:], h2s[:, h * 64:(h + 1) * 64],
                    start=True, stop=True),
                    reads=[("h2s",), ("ones16f",)], writes=[K(dc)])
                p.op("dve", lambda e, h=h, dc=dc: e.tensor_mul(
                    S[:, h * 64:(h + 1) * 64], S[:, h * 64:(h + 1) * 64],
                    dc[0:64, 0:64]),
                    reads=[("S",), K(dc)], writes=[("S",)])
                p.op("dve", lambda e, h=h, ds=ds: e.tensor_add(
                    S[:, h * 64:(h + 1) * 64], S[:, h * 64:(h + 1) * 64],
                    ds[0:64, 0:64]),
                    reads=[("S",), K(ds)], writes=[("S",)])

        env.ones16f = sbuf(nc, es, "ones16f", (16, 64), F32)
        p.op("dve", lambda e: e.memset(env.ones16f[:], 1.0), writes=[("ones16f",)])

        def finish_tile(tt):
            for tc in range(4):
                for n2 in range(2):
                    bank = env.psum()
                    for j in range(NC):
                        p.op("pe", lambda e, tc=tc, n2=n2, j=j, tt=tt, bank=bank: e.matmul(
                            bank[:, 0:512],
                            ub[:, j * TT + tc * 128:j * TT + (tc + 1) * 128],
                            wgs[:, j * 1024 + n2 * 512:j * 1024 + (n2 + 1) * 512],
                            start=(j == 0), stop=(j == NC - 1)),
                            reads=[("ub", j), ("wgs",)], writes=[K(bank)])
                    p.op("act", lambda e, n2=n2, bank=bank: e.activation(
                        out=gtm[:, n2 * 512:(n2 + 1) * 512], in_=bank[:, 0:512],
                        func=AF.Silu),
                        reads=[K(bank)], writes=[("gtm",)])
                for h in range(16):
                    osl = otm[:, tc * 1024 + h * 64:tc * 1024 + (h + 1) * 64]
                    p.op("dve", lambda e, osl=osl: e.tensor_mul(scv[:, 0:64], osl, osl),
                         reads=[("otm", tc, h)], writes=[("scv",)])
                    p.op("dve", lambda e: e.tensor_reduce(colw[:, 0:1], scv[:, 0:64],
                                                          AX.X, ALU.add),
                         reads=[("scv",)], writes=[("colw",)])
                    p.op("dve", lambda e: e.tensor_scalar(colw[:, 1:2], colw[:, 0:1],
                                                          1.0 / 64.0, None, ALU.mult),
                         reads=[("colw",)], writes=[("colw",)])
                    p.op("dve", lambda e: e.tensor_scalar_add(colw[:, 1:2], colw[:, 1:2],
                                                              1e-5),
                         reads=[("colw",)], writes=[("colw",)])
                    p.op("act", lambda e: e.activation(out=colw[:, 1:2], in_=colw[:, 1:2],
                                                       func=AF.Sqrt),
                         reads=[("colw",)], writes=[("colw",)])
                    p.op("dve", lambda e: e.reciprocal(colw[:, 1:2], colw[:, 1:2]),
                         reads=[("colw",)], writes=[("colw",)])
                    p.op("dve", lambda e, osl=osl: e.tensor_scalar(osl, osl, colw[:, 1:2],
                                                                   None, ALU.mult),
                         reads=[("otm", tc, h), ("colw",)], writes=[("otm", tc, h)])
                    p.op("dve", lambda e, osl=osl, h=h: e.tensor_mul(
                        osl, osl, env.nwrow[:, 0:64]),
                        reads=[("otm", tc, h), ("nwrow",)], writes=[("otm", tc, h)])
                    p.op("dve", lambda e, osl=osl, h=h: e.tensor_mul(
                        osl, osl, gtm[:, h * 64:(h + 1) * 64]),
                        reads=[("otm", tc, h), ("gtm",)], writes=[("otm", tc, h)])
                for fb in range(8):
                    p.op("pe", lambda e, tc=tc, fb=fb: e.transpose(
                        env.ps_bf[:, 0:128], otm[:, tc * 1024 + fb * 128:tc * 1024 + (fb + 1) * 128],
                        env.identb[:]),
                        reads=[("otm", tc, 2 * fb), ("otm", tc, 2 * fb + 1), ("identb",)],
                        writes=[K(env.ps_bf)])
                    p.op("dve", lambda e, tc=tc, fb=fb: e.tensor_copy(
                        ofm[:, fb * TT + tc * 128:fb * TT + (tc + 1) * 128],
                        env.ps_bf[:, 0:128]),
                        reads=[K(env.ps_bf)], writes=[("ofm", fb)])
            def epo(m, bank):
                p.op("dve", lambda e, m=m, bank=bank: e.tensor_add(
                    xt[:, m * TT:(m + 1) * TT], bank[:, 0:TT], xt[:, m * TT:(m + 1) * TT]),
                    reads=[K(bank), ("xt", m)], writes=[("xt", m)])
                p.dma("sp", lambda e, m=m, tt=tt: e.dma_start(
                    out=y_h[m * 128:(m + 1) * 128, tt * TT:(tt + 1) * TT],
                    in_=xt[:, m * TT:(m + 1) * TT]),
                    reads=[("xt", m)], sem="dy")
            stream_gemm(p, env, wpre_h=wo_h, nk=8, nm=8,
                        rhs_ap=lambda k: ofm[:, k * TT:(k + 1) * TT],
                        rhs_keys=lambda k: [("ofm", k)],
                        out_fn=epo, slots=wsl, slot_base="wsl", sem="dwo", tag=f"op{tt}")

        env.nwrow = sbuf(nc, es, "nwrow", (128, 64), BF16)
        p.dma("pool", lambda e: e.dma_start(out=env.nwrow[:], in_=nw_h[:, :]),
              writes=[("nwrow",)], sem="dcnw")

        def halo_prep():
            for c in range(NC):
                p.dma("pool", lambda e, c=c: e.dma_start(
                    out=halo3[:, c * 3:(c + 1) * 3],
                    in_=xhalo_h[c * 128:(c + 1) * 128, 1:4]),
                    writes=[("halo3",)], sem="dh3")
            for t in range(3):
                tok = halo3[:].rearrange("p (c t) -> p t c", t=3)[:, t, :]
                col_ln(p, env, tok, ("halo3",))
            p.op("dve", lambda e: e.tensor_copy(scv[:, 0:24], halo3[:]),
                 reads=[("halo3",)], writes=[("scv",)])
            for wname, wh, dst, dkey in (("q", wq_h, qm, "qm"), ("k", wk_h, km, "km"),
                                         ("v", wv_h, vm, "vm")):
                for m in range(NC):
                    slot = wsl[m % 2]
                    p.dma("pool", lambda e, m=m, wh=wh, slot=slot: e.dma_start(
                        out=slot[:].rearrange("p (k j) -> p k j", j=128), in_=wh[m]),
                        writes=[("wsl", m % 2)], sem=f"dw{m % 2}")
                    bank = env.psum()
                    for j in range(NC):
                        p.op("pe", lambda e, m=m, j=j, slot=slot, bank=bank: e.matmul(
                            bank[:, 0:3], slot[:, j * 128:(j + 1) * 128],
                            scv[:, j * 3:(j + 1) * 3],
                            start=(j == 0), stop=(j == NC - 1)),
                            reads=[("wsl", m % 2), ("scv",)], writes=[K(bank)])
                    p.op("dve", lambda e, m=m, dst=dst, dkey=dkey, bank=bank: e.tensor_scalar(
                        dst[:, m * MB:m * MB + 3], bank[:, 0:3], hmask[:, 0:1],
                        None, ALU.mult),
                        reads=[K(bank), ("hmask",)], writes=[(dkey, m)])

        halo_prep()
        for tt in range(NTT):
            proj_conv(tt)
            l2n(qm, "qm", 0.125)
            l2n(km, "km", 1.0)
            ab_rows(tt)
            for ch in range(NCH):
                scan_chunk(tt, ch)
            finish_tile(tt)
        p.dma("sp", lambda e: e.dma_start(out=st_h[:, :], in_=S[:]),
              reads=[("S",)], sem="dy")
        if shared is None:
            p.finalize(es)
    return nc



def shard_x(x):
    out = []
    for c in range(NCORES):
        b, h = c // 2, c % 2
        out.append(np.ascontiguousarray(x[b, h * TCORE:(h + 1) * TCORE, :].T))
    return out


def unshard_x(parts):
    x = np.empty((4, 8192, 1024), np.float32)
    for c in range(NCORES):
        b, h = c // 2, c % 2
        x[b, h * TCORE:(h + 1) * TCORE, :] = parts[c].T
    return x


def vec_pc(v):
    n = v.shape[0] // 128
    return np.ascontiguousarray(v.reshape(n, 128).T)


def w_pre(w, mj=128):
    din, dout = w.shape
    nk, nm = din // 128, dout // mj
    return np.ascontiguousarray(w.reshape(nk, 128, nm, mj).transpose(2, 1, 0, 3))


_prog_cache = {}


def get_prog(name, builder):
    if name not in _prog_cache:
        _prog_cache[name] = builder()
    return _prog_cache[name]


def run_l1ffn(x_full, w1, b1, w2, b2):
    nc = get_prog("l1ffn", build_l1ffn)
    xp = shard_x(x_full)
    w1p, w2p = w_pre(w1), w_pre(w2)
    b1p, b2p = vec_pc(b1), vec_pc(b2)
    in_maps = [{"x": xp[c], "w1": w1p, "b1": b1p, "w2": w2p, "b2": b2p}
               for c in range(NCORES)]
    res = run_bass_kernel_spmd(nc, in_maps, core_ids=list(range(NCORES)))
    return unshard_x([np.asarray(res.results[c]["y"]) for c in range(NCORES)])




def _l0ffn_consts(h, tcore=4096):
    DIM_HEAD, WINDOW, NSEQ = 64, 512, 8192
    half_idx = np.arange(0, DIM_HEAD, 2, dtype=np.float64)
    inv_freq = 1.0 / 10000.0 ** (half_idx / DIM_HEAD)
    base = (half_idx + 0.4 * DIM_HEAD) / (1.4 * DIM_HEAD)
    pos = np.arange(h * tcore - 513, h * tcore + tcore, dtype=np.float64)
    freqs = pos[:, None] * inv_freq[None, :]
    freqs = np.concatenate([freqs, freqs], axis=1)
    power = (pos - NSEQ // 2) / float(WINDOW // 2)
    scale = base[None, :] ** power[:, None]
    scale = np.concatenate([scale, scale], axis=1)
    cos, sin = np.cos(freqs), np.sin(freqs)
    sgn = np.concatenate([-np.ones(32), np.ones(32)])
    csq = (cos * scale).T
    ssq = (sin * scale * sgn[None, :]).T
    csk = (cos / scale).T
    ssk = (sin / scale * sgn[None, :]).T
    rot = np.concatenate([np.tile(csq, (2, 1)), np.tile(ssq, (2, 1)),
                          np.tile(csk, (2, 1)), np.tile(ssk, (2, 1))], axis=1)
    rot = np.ascontiguousarray(rot.astype(np.float32))
    swp = np.zeros((128, 128), np.float32)
    for r in range(128):
        blk, rr = r // 64 * 64, r % 64
        c = blk + (rr + 32 if rr < 32 else rr - 32)
        swp[c, r] = 1.0
    ii = np.arange(128)
    trim0 = np.where(ii[:, None] > ii[None, :], -1e30, 0.0).astype(np.float32)
    trim1 = np.where(ii[None, :] > ii[:, None], -1e30, 0.0).astype(np.float32)
    trim = np.concatenate([trim0, trim1], axis=1)
    e2a = np.zeros((128, 2), np.float32)
    e2a[:64, 0] = 1; e2a[64:, 1] = 1
    e2b = np.zeros((2, 128), np.float32)
    e2b[0, :64] = 1; e2b[1, 64:] = 1
    return rot, swp, trim, e2a, e2b


def _l0ffn_inmaps(xparts, tails, xres_parts, prm, l, first):
    wqkv = prm["wqkv"][l]
    wqk = w_pre(np.ascontiguousarray(wqkv[:, 0:1024]))
    wv = np.ascontiguousarray(wqkv[:, 1024:1536].reshape(8, 128, 512).transpose(1, 0, 2))
    qsc = np.zeros((128, 2), np.float32)
    qsc[:, 0] = np.tile(prm["q_scale"][l], 2) * 8.0
    qsc[:, 1] = np.tile(prm["k_scale"][l], 2)
    wo = w_pre(np.ascontiguousarray(prm["wo"][l]))
    lgb = np.concatenate([vec_pc(prm["ln_g"][l]), vec_pc(prm["ln_b"][l])], axis=1)
    w1 = prm["w1"][l]
    HID = 2730
    blocks, b1cols = [], []
    for k in range(22):
        for side in range(2):
            lo = k * 128 + side * HID
            hi = min((k + 1) * 128, HID) + side * HID
            blk = np.zeros((1024, 128), np.float32)
            blk[:, 0:hi - lo] = w1[:, lo:hi]
            blocks.append(blk.reshape(8, 128, 128).transpose(1, 0, 2))
            bc = np.zeros(128, np.float32)
            bc[0:hi - lo] = prm["b1"][l][lo:hi]
            b1cols.append(bc)
    w1p = np.ascontiguousarray(np.stack([b for pair in zip(blocks[0::2], blocks[1::2])
                                         for b in pair]))
    w1p = np.ascontiguousarray(np.stack(blocks))
    b1p = np.ascontiguousarray(np.stack(b1cols, axis=1))
    w2 = np.zeros((2816, 1024), np.float32)
    w2[0:HID] = prm["w2"][l]
    w2p = w_pre(w2)
    b2p = vec_pc(prm["b2"][l])
    in_maps = []
    for c in range(NCORES):
        h = c % 2
        rot, swp, trim, e2a, e2b = _l0ffn_consts(h)
        pm = np.full((128, 1), 0.0 if h == 1 else -1e30, np.float32)
        in_maps.append(dict(x=xparts[c], xtail=tails[c], xres=xres_parts[c], pm=pm,
                            wqk=wqk, wv=wv, qsc=qsc, wo=wo, lgb=lgb, w1=w1p, b1=b1p,
                            w2=w2p, b2=b2p, rot=rot, swp=swp, e2a=e2a, e2b=e2b,
                            trim=trim))
    return in_maps


def _tails_from(parts):
    tails = []
    for c in range(NCORES):
        if c % 2 == 0:
            tails.append(np.zeros((D, 513), np.float32))
        else:
            tails.append(np.ascontiguousarray(parts[c - 1][:, TCORE - 513:]))
    return tails


def run_l0ffn(x_full, prm):
    ncA = get_prog("l0ffnA", lambda: build_l0ffn_layer(True))
    ncB = get_prog("l0ffnB", lambda: build_l0ffn_layer(False))
    xp = shard_x(x_full)
    zeros = [np.zeros((D, TCORE), np.float32)] * NCORES
    im1 = _l0ffn_inmaps(xp, _tails_from(xp), zeros, prm, 0, True)
    r1 = run_bass_kernel_spmd(ncA, im1, core_ids=list(range(NCORES)))
    xi1 = [np.ascontiguousarray(np.asarray(r1.results[c]["y"])) for c in range(NCORES)]
    im2 = _l0ffn_inmaps(xi1, _tails_from(xi1), xp, prm, 1, False)
    r2 = run_bass_kernel_spmd(ncB, im2, core_ids=list(range(NCORES)))
    return unshard_x([np.asarray(r2.results[c]["y"]) for c in range(NCORES)])




def run_mamba(x_full, prm):
    ncA = get_prog("mambaA", lambda: build_mamba_layer(True))
    ncB = get_prog("mambaB", lambda: build_mamba_layer(False))
    xp = shard_x(x_full)
    zeros = [np.zeros((D, TCORE), np.float32)] * NCORES
    zS = np.zeros((64, 2048), np.float32)
    ehf = np.zeros((32, 32 * 128), np.float32)
    for h in range(32):
        ehf[h, h * 128:(h + 1) * 128] = 1.0
    h2c = np.zeros((32, 2048), np.float32)
    for c in range(2048):
        h2c[c // 64, c] = 1.0
    ii = np.arange(128)
    maskt = (ii[None, :] >= ii[:, None]).astype(np.float32)
    shf = np.zeros((128, 64), np.float32)
    for r in range(64):
        shf[r + 64, r] = 1.0

    def inmaps(l, xparts, tails, xres, Scarries):
        inw = np.zeros((1024, 4352), np.float32)
        inw[:, 0:4256] = prm["in_proj_w"][l].T
        inwp = w_pre(inw)
        cw = np.ascontiguousarray(prm["conv_w"][l].reshape(17, 128, 4)
                                  .transpose(1, 0, 2).reshape(128, 68))
        cb = vec_pc(prm["conv_b"][l])
        hv = np.stack([prm["dt_bias"][l], prm["A_log"][l]], axis=1).astype(np.float32)
        drms = np.zeros((128, 32), np.float32)
        for cc in range(16):
            drms[:, cc] = prm["D"][l][(cc * 128 + np.arange(128)) // 64]
        drms[:, 16:32] = vec_pc(prm["rms_w"][l])
        outw = w_pre(np.ascontiguousarray(prm["out_proj_w"][l].T))
        ims = []
        for c in range(NCORES):
            hmask = np.full((128, 1), 1.0 if c % 2 == 1 else 0.0, np.float32)
            ims.append(dict(x=xparts[c], xhalo=tails[c], xres=xres[c], lnwb=lnwbs[l],
                            inw=inwp, convw=cw, convb=cb, hvec=hv, drms=drms,
                            outw=outw, sin=Scarries[c], ehf=ehf, h2c=h2c,
                            maskt=maskt, hmask=hmask, shf=shf))
        return ims

    lnwbs = [np.concatenate([vec_pc(prm["ln_w"][l]), vec_pc(prm["ln_b"][l])], axis=1)
             for l in range(2)]

    def tails3(parts):
        t = []
        for c in range(NCORES):
            a = np.zeros((D, 4), np.float32)
            if c % 2 == 1:
                a[:, 1:4] = parts[c - 1][:, TCORE - 3:]
            t.append(a)
        return t

    cur = xp
    for l in range(2):
        ncx = ncA if l == 0 else ncB
        xres = zeros if l == 0 else xp
        zSl = [zS] * NCORES
        im1 = inmaps(l, cur, tails3(cur), xres, zSl)
        r1 = run_bass_kernel_spmd(ncx, im1, core_ids=list(range(NCORES)))
        Sc = []
        for c in range(NCORES):
            if c % 2 == 1:
                Sc.append(np.ascontiguousarray(np.asarray(r1.results[c - 1]["st"])))
            else:
                Sc.append(zS)
        im2 = inmaps(l, cur, tails3(cur), xres, Sc)
        r2 = run_bass_kernel_spmd(ncx, im2, core_ids=list(range(NCORES)))
        cur = [np.ascontiguousarray(np.asarray(r2.results[c]["y"])) for c in range(NCORES)]
    return unshard_x(cur)




def run_delta(x_full, prm):
    ncD = get_prog("delta", build_delta)
    xp = shard_x(x_full)
    zS = np.zeros((64, 1024), np.float32)
    ehf = np.zeros((16, 16 * 128), np.float32)
    for h in range(16):
        ehf[h, h * 128:(h + 1) * 128] = 1.0
    h2c = np.zeros((16, 1024), np.float32)
    for c in range(1024):
        h2c[c // 64, c] = 1.0
    ii = np.arange(128)
    mneg = np.where(ii[:, None] < ii[None, :], -1.0, 0.0).astype(np.float32)
    mpos = np.where(ii[:, None] <= ii[None, :], 1.0, 0.0).astype(np.float32)

    def pad128(w):
        out = np.zeros((1024, 128), np.float32)
        out[:, 0:w.shape[1]] = w
        return out

    wab = np.stack([w_pre(pad128(prm["Wb"].T))[0], w_pre(pad128(prm["Wa"].T))[0]])
    cqkv = np.concatenate([
        prm["conv_q"].reshape(8, 128, 4).transpose(1, 0, 2).reshape(128, 32),
        prm["conv_k"].reshape(8, 128, 4).transpose(1, 0, 2).reshape(128, 32),
        prm["conv_v"].reshape(8, 128, 4).transpose(1, 0, 2).reshape(128, 32)], axis=1)
    cqkv = np.ascontiguousarray(cqkv)
    hv = np.stack([prm["dt_bias"], prm["A_log"]], axis=1).astype(np.float32)
    nw = np.ascontiguousarray(np.tile(prm["norm_w"][None, :], (128, 1)).astype(np.float32))
    wts = dict(wq=w_pre(np.ascontiguousarray(prm["Wq"].T)),
               wk=w_pre(np.ascontiguousarray(prm["Wk"].T)),
               wv=w_pre(np.ascontiguousarray(prm["Wv"].T)),
               wg=np.ascontiguousarray(prm["Wg"].T.reshape(8, 128, 1024).transpose(1, 0, 2)),
               wo=w_pre(np.ascontiguousarray(prm["Wo"].T)),
               wab=wab, cqkv=cqkv, hv=hv, nw=nw, ehf=ehf, h2c=h2c,
               mneg=mneg, mpos=mpos)

    def tails3(parts):
        t = []
        for c in range(NCORES):
            a = np.zeros((D, 4), np.float32)
            if c % 2 == 1:
                a[:, 1:4] = parts[c - 1][:, TCORE - 3:]
            t.append(a)
        return t

    e2ad = np.zeros((128, 2), np.float32)
    e2ad[:64, 0] = 1; e2ad[64:, 1] = 1
    e2bd = np.zeros((2, 128), np.float32)
    e2bd[0, :64] = 1; e2bd[1, 64:] = 1
    wts["e2ad"] = e2ad; wts["e2bd"] = e2bd
    tls = tails3(xp)
    ims = []
    for c in range(NCORES):
        hm = np.full((128, 1), 1.0 if c % 2 == 1 else 0.0, np.float32)
        ims.append(dict(x=xp[c], xhalo=tls[c], sin=zS, hmask=hm, **wts))
    r1 = run_bass_kernel_spmd(ncD, ims, core_ids=list(range(NCORES)))
    ims2 = []
    for c in range(NCORES):
        hm = np.full((128, 1), 1.0 if c % 2 == 1 else 0.0, np.float32)
        Sc = (np.ascontiguousarray(np.asarray(r1.results[c - 1]["st"]))
              if c % 2 == 1 else zS)
        ims2.append(dict(x=xp[c], xhalo=tls[c], sin=Sc, hmask=hm, **wts))
    r2 = run_bass_kernel_spmd(ncD, ims2, core_ids=list(range(NCORES)))
    return unshard_x([np.asarray(r2.results[c]["y"]) for c in range(NCORES)])





def build_fused(nblocks=11):
    global TCORE, NTT
    oldT, oldN = TCORE, NTT
    TCORE, NTT = 8192, 16
    try:
        nc = bass.Bass("TRN2", target_bir_lowering=False)
        p = Prog(nc)
        x0 = nc.dram_tensor("x", [D, TCORE], F32, kind="ExternalInput").ap()
        y = nc.dram_tensor("y", [D, TCORE], F32, kind="ExternalOutput").ap()
        t = [nc.dram_tensor(f"tbuf{i}", [D, TCORE], F32, kind="Internal").ap()
             for i in range(10)]
        sh = (nc, p)
        chain = [
            lambda o, d: build_mamba_layer(True, shared=sh, x_ap=o, y_ap=d, xres_ap=x0, pfx="ma_"),
            lambda o, d: build_mamba_layer(False, shared=sh, x_ap=o, y_ap=d, xres_ap=x0, pfx="mb_"),
            lambda o, d: build_l0ffn_layer(True, shared=sh, x_ap=o, y_ap=d, xres_ap=x0, pfx="fa_"),
            lambda o, d: build_l0ffn_layer(False, shared=sh, x_ap=o, y_ap=d, xres_ap=t[1], pfx="fb_"),
            lambda o, d: build_l0attnb(shared=sh, x_ap=o, y_ap=d, pfx="ab_"),
            lambda o, d: build_l0ffn_layer(True, shared=sh, x_ap=o, y_ap=d, xres_ap=x0, pfx="fc_"),
            lambda o, d: build_l0ffn_layer(False, shared=sh, x_ap=o, y_ap=d, xres_ap=t[4], pfx="fd_"),
            lambda o, d: build_delta(shared=sh, x_ap=o, y_ap=d, pfx="da_"),
            lambda o, d: build_l1ffn(shared=sh, x_ap=o, y_ap=d, pfx="ea_"),
            lambda o, d: build_l1attnb(shared=sh, x_ap=o, y_ap=d, pfx="ra_"),
            lambda o, d: build_l1ffn(shared=sh, x_ap=o, y_ap=d, pfx="eb_"),
        ]
        cur = x0
        for i in range(nblocks):
            dst = y if i == nblocks - 1 else t[i]
            chain[i](cur, dst)
            if i != nblocks - 1:
                p.barrier()
            cur = dst
        with contextlib.ExitStack() as es:
            p.finalize(es)
        return nc
    finally:
        TCORE, NTT = oldT, oldN


def _prep_mamba(prm, l):
    ehf = np.zeros((32, 32 * 128), np.float32)
    for h in range(32):
        ehf[h, h * 128:(h + 1) * 128] = 1.0
    h2c = np.zeros((32, 2048), np.float32)
    for c in range(2048):
        h2c[c // 64, c] = 1.0
    ii = np.arange(128)
    maskt = (ii[None, :] >= ii[:, None]).astype(np.float32)
    shf = np.zeros((128, 64), np.float32)
    for r in range(64):
        shf[r + 64, r] = 1.0
    inw = np.zeros((1024, 4352), np.float32)
    inw[:, 0:4256] = prm["in_proj_w"][l].T
    cw = np.ascontiguousarray(prm["conv_w"][l].reshape(17, 128, 4)
                              .transpose(1, 0, 2).reshape(128, 68))
    hv = np.stack([prm["dt_bias"][l], prm["A_log"][l]], axis=1).astype(np.float32)
    drms = np.zeros((128, 32), np.float32)
    for cc in range(16):
        drms[:, cc] = prm["D"][l][(cc * 128 + np.arange(128)) // 64]
    drms[:, 16:32] = vec_pc(prm["rms_w"][l])
    lnwb = np.concatenate([vec_pc(prm["ln_w"][l]), vec_pc(prm["ln_b"][l])], axis=1)
    return dict(lnwb=np.ascontiguousarray(lnwb), inw=w_pre(inw), convw=cw,
                convb=vec_pc(prm["conv_b"][l]), hvec=hv, drms=drms,
                outw=w_pre(np.ascontiguousarray(prm["out_proj_w"][l].T)),
                sin=np.zeros((64, 2048), np.float32), ehf=ehf, h2c=h2c,
                maskt=maskt, hmask=np.zeros((128, 1), np.float32), shf=shf,
                xhalo=np.zeros((D, 4), np.float32))


def _prep_l0ffn(prm, l, tcore):
    wqkv = prm["wqkv"][l]
    wqk = w_pre(np.ascontiguousarray(wqkv[:, 0:1024]))
    wv = np.ascontiguousarray(wqkv[:, 1024:1536].reshape(8, 128, 512).transpose(1, 0, 2))
    qsc = np.zeros((128, 2), np.float32)
    qsc[:, 0] = np.tile(prm["q_scale"][l], 2) * 8.0
    qsc[:, 1] = np.tile(prm["k_scale"][l], 2)
    wo = w_pre(np.ascontiguousarray(prm["wo"][l]))
    lgb = np.concatenate([vec_pc(prm["ln_g"][l]), vec_pc(prm["ln_b"][l])], axis=1)
    w1 = prm["w1"][l]
    HID = 2730
    blocks, b1cols = [], []
    for k in range(22):
        for side in range(2):
            lo = k * 128 + side * HID
            hi = min((k + 1) * 128, HID) + side * HID
            blk = np.zeros((1024, 128), np.float32)
            blk[:, 0:hi - lo] = w1[:, lo:hi]
            blocks.append(blk.reshape(8, 128, 128).transpose(1, 0, 2))
            bc = np.zeros(128, np.float32)
            bc[0:hi - lo] = prm["b1"][l][lo:hi]
            b1cols.append(bc)
    w1p = np.ascontiguousarray(np.stack(blocks))
    b1p = np.ascontiguousarray(np.stack(b1cols, axis=1))
    w2 = np.zeros((2816, 1024), np.float32)
    w2[0:HID] = prm["w2"][l]
    rot, swp, trim, e2a, e2b = _l0ffn_consts(0, tcore)
    return dict(wqk=wqk, wv=wv, qsc=qsc, wo=wo, lgb=np.ascontiguousarray(lgb),
                w1=w1p, b1=b1p, w2=w_pre(w2), b2=vec_pc(prm["b2"][l]),
                rot=rot, swp=swp, e2a=e2a, e2b=e2b, trim=trim,
                pm=np.full((128, 1), -1e30, np.float32),
                xtail=np.zeros((D, 513), np.float32))


def _prep_attnb(I):
    wq = w_pre(np.ascontiguousarray(np.asarray(I["L0_attn_b_Wq"], np.float32).T))
    wk = w_pre(np.ascontiguousarray(np.asarray(I["L0_attn_b_Wk"], np.float32).T))
    wv = np.ascontiguousarray(np.asarray(I["L0_attn_b_Wv"], np.float32).T
                              .reshape(8, 128, 1024).transpose(1, 0, 2))
    return dict(wq=wq, wk=wk, wv=wv,
                bq=vec_pc(np.asarray(I["L0_attn_b_bq"], np.float32)),
                bk=vec_pc(np.asarray(I["L0_attn_b_bk"], np.float32)),
                bv=np.asarray(I["L0_attn_b_bv"], np.float32).reshape(1, 1024),
                gm=vec_pc(np.asarray(I["L0_attn_b_gamma"], np.float32)),
                bt=vec_pc(np.asarray(I["L0_attn_b_beta"], np.float32)))


def _prep_delta(prm):
    ehf = np.zeros((16, 16 * 128), np.float32)
    for h in range(16):
        ehf[h, h * 128:(h + 1) * 128] = 1.0
    h2c = np.zeros((16, 1024), np.float32)
    for c in range(1024):
        h2c[c // 64, c] = 1.0
    ii = np.arange(128)
    mneg = np.where(ii[:, None] < ii[None, :], -1.0, 0.0).astype(np.float32)
    mpos = np.where(ii[:, None] <= ii[None, :], 1.0, 0.0).astype(np.float32)

    def pad128(w):
        out = np.zeros((1024, 128), np.float32)
        out[:, 0:w.shape[1]] = w
        return out

    wab = np.stack([w_pre(pad128(prm["Wb"].T))[0], w_pre(pad128(prm["Wa"].T))[0]])
    cqkv = np.ascontiguousarray(np.concatenate([
        prm["conv_q"].reshape(8, 128, 4).transpose(1, 0, 2).reshape(128, 32),
        prm["conv_k"].reshape(8, 128, 4).transpose(1, 0, 2).reshape(128, 32),
        prm["conv_v"].reshape(8, 128, 4).transpose(1, 0, 2).reshape(128, 32)], axis=1))
    hv = np.stack([prm["dt_bias"], prm["A_log"]], axis=1).astype(np.float32)
    e2ad = np.zeros((128, 2), np.float32)
    e2ad[:64, 0] = 1; e2ad[64:, 1] = 1
    e2bd = np.zeros((2, 128), np.float32)
    e2bd[0, :64] = 1; e2bd[1, 64:] = 1
    return dict(wq=w_pre(np.ascontiguousarray(prm["Wq"].T)),
                wk=w_pre(np.ascontiguousarray(prm["Wk"].T)),
                wv=w_pre(np.ascontiguousarray(prm["Wv"].T)),
                wg=np.ascontiguousarray(prm["Wg"].T.reshape(8, 128, 1024).transpose(1, 0, 2)),
                wo=w_pre(np.ascontiguousarray(prm["Wo"].T)),
                wab=wab, cqkv=cqkv, hv=hv,
                nw=np.ascontiguousarray(np.tile(prm["norm_w"][None, :], (128, 1)).astype(np.float32)),
                ehf=ehf, h2c=h2c, mneg=mneg, mpos=mpos,
                sin=np.zeros((64, 1024), np.float32),
                hmask=np.zeros((128, 1), np.float32),
                xhalo=np.zeros((D, 4), np.float32),
                e2ad=e2ad, e2bd=e2bd)


def _prep_rwkv(prm):
    vecs = np.concatenate([vec_pc(prm[n]) for n in
                           ["ln0_w", "ln0_b", "ln1_w", "ln1_b", "ln2_w", "ln2_b",
                            "time_decay", "time_first", "tmk", "tmv", "tmr",
                            "cmk", "cmr"]], axis=1).astype(np.float32)
    zci = np.zeros((128, 24), np.float32)
    zci[:, 16:24] = -1e38
    return dict(vecs=np.ascontiguousarray(vecs), ci=zci,
                hhal=np.zeros((128, 16), np.float32),
                wk=w_pre(np.ascontiguousarray(prm["Wk"].T)),
                wv=w_pre(np.ascontiguousarray(prm["Wv"].T)),
                wr=w_pre(np.ascontiguousarray(prm["Wr"].T)),
                wo=w_pre(np.ascontiguousarray(prm["Wo"].T)),
                wfk=w_pre(np.ascontiguousarray(prm["Wfk"].T)),
                wfv=w_pre(np.ascontiguousarray(prm["Wfv"].T)),
                wfr=w_pre(np.ascontiguousarray(prm["Wfr"].T)))


def _prep_l1ffn(I):
    return dict(w1=w_pre(np.ascontiguousarray(np.asarray(I["L1_ffn_w1"], np.float32))),
                b1=vec_pc(np.asarray(I["L1_ffn_b1"], np.float32)),
                w2=w_pre(np.ascontiguousarray(np.asarray(I["L1_ffn_w2"], np.float32))),
                b2=vec_pc(np.asarray(I["L1_ffn_b2"], np.float32)))


def kernel_fused(I, nblocks=11):
    nc = get_prog(f"fused{nblocks}", lambda: build_fused(nblocks))
    x = np.asarray(I["L0_attn_a_x"], np.float32)
    mprm = {k: np.asarray(I["L0_attn_a_" + k], np.float32) for k in
            ["ln_w", "ln_b", "in_proj_w", "conv_w", "conv_b", "dt_bias",
             "A_log", "D", "rms_w", "out_proj_w"]}
    fprm = {k: np.asarray(I["L0_ffn_" + k], np.float32) for k in
            ["wqkv", "q_scale", "k_scale", "wo", "ln_g", "ln_b", "w1", "b1",
             "w2", "b2"]}
    dprm = {k: np.asarray(I["L1_attn_a_" + k], np.float32) for k in
            ["Wq", "Wk", "Wv", "conv_q", "conv_k", "conv_v", "Wb", "Wa",
             "A_log", "dt_bias", "Wg", "norm_w", "Wo"]}
    rprm = {k: np.asarray(I["L1_attn_b_" + k], np.float32) for k in
            ["ln0_w", "ln0_b", "ln1_w", "ln1_b", "ln2_w", "ln2_b", "time_decay",
             "time_first", "tmk", "tmv", "tmr", "Wk", "Wv", "Wr", "Wo", "cmk",
             "cmr", "Wfk", "Wfr", "Wfv"]}
    blocks = {}
    blocks["ma_"] = _prep_mamba(mprm, 0)
    mb = dict(_prep_mamba(mprm, 1))
    blocks["mb_"] = mb
    fa = _prep_l0ffn(fprm, 0, 8192)
    fb = _prep_l0ffn(fprm, 1, 8192)
    blocks["fa_"], blocks["fb_"] = fa, fb
    blocks["ab_"] = _prep_attnb(I)
    blocks["fc_"], blocks["fd_"] = fa, fb
    blocks["da_"] = _prep_delta(dprm)
    blocks["ea_"] = _prep_l1ffn(I)
    blocks["ra_"] = _prep_rwkv(rprm)
    blocks["eb_"] = blocks["ea_"]
    base = {}
    for pfx, d in blocks.items():
        for k, v in d.items():
            base[pfx + k] = v
    in_maps = []
    for c in range(4):
        m = dict(base)
        m["x"] = np.ascontiguousarray(x[c].T)
        in_maps.append(m)
    res = run_bass_kernel_spmd(nc, in_maps, core_ids=[0, 1, 2, 3])
    out = np.empty((4, 8192, 1024), np.float32)
    for c in range(4):
        out[c] = np.asarray(res.results[c]["y"]).T
    return out


def kernel(**inputs):
    return kernel_fused(inputs)


def kernel_unfused(**inputs):
    I = inputs
    x = np.asarray(I["L0_attn_a_x"], np.float32)

    mamba_prm = dict(ln_w=I["L0_attn_a_ln_w"], ln_b=I["L0_attn_a_ln_b"],
                     in_proj_w=I["L0_attn_a_in_proj_w"], conv_w=I["L0_attn_a_conv_w"],
                     conv_b=I["L0_attn_a_conv_b"], dt_bias=I["L0_attn_a_dt_bias"],
                     A_log=I["L0_attn_a_A_log"], D=I["L0_attn_a_D"],
                     rms_w=I["L0_attn_a_rms_w"], out_proj_w=I["L0_attn_a_out_proj_w"])
    ffn_prm = dict(wqkv=I["L0_ffn_wqkv"], q_scale=I["L0_ffn_q_scale"],
                   k_scale=I["L0_ffn_k_scale"], wo=I["L0_ffn_wo"],
                   ln_g=I["L0_ffn_ln_g"], ln_b=I["L0_ffn_ln_b"], w1=I["L0_ffn_w1"],
                   b1=I["L0_ffn_b1"], w2=I["L0_ffn_w2"], b2=I["L0_ffn_b2"])
    delta_prm = dict(Wq=I["L1_attn_a_Wq"], Wk=I["L1_attn_a_Wk"], Wv=I["L1_attn_a_Wv"],
                     conv_q=I["L1_attn_a_conv_q"], conv_k=I["L1_attn_a_conv_k"],
                     conv_v=I["L1_attn_a_conv_v"], Wb=I["L1_attn_a_Wb"],
                     Wa=I["L1_attn_a_Wa"], A_log=I["L1_attn_a_A_log"],
                     dt_bias=I["L1_attn_a_dt_bias"], Wg=I["L1_attn_a_Wg"],
                     norm_w=I["L1_attn_a_norm_w"], Wo=I["L1_attn_a_Wo"])
    rwkv_prm = dict(ln0_w=I["L1_attn_b_ln0_w"], ln0_b=I["L1_attn_b_ln0_b"],
                    ln1_w=I["L1_attn_b_ln1_w"], ln1_b=I["L1_attn_b_ln1_b"],
                    ln2_w=I["L1_attn_b_ln2_w"], ln2_b=I["L1_attn_b_ln2_b"],
                    time_decay=I["L1_attn_b_time_decay"], time_first=I["L1_attn_b_time_first"],
                    tmk=I["L1_attn_b_tmk"], tmv=I["L1_attn_b_tmv"], tmr=I["L1_attn_b_tmr"],
                    Wk=I["L1_attn_b_Wk"], Wv=I["L1_attn_b_Wv"], Wr=I["L1_attn_b_Wr"],
                    Wo=I["L1_attn_b_Wo"], cmk=I["L1_attn_b_cmk"], cmr=I["L1_attn_b_cmr"],
                    Wfk=I["L1_attn_b_Wfk"], Wfr=I["L1_attn_b_Wfr"], Wfv=I["L1_attn_b_Wfv"])

    prm = {k: np.asarray(v, np.float32) for k, v in mamba_prm.items()}
    x = run_mamba(x, prm)
    prm = {k: np.asarray(v, np.float32) for k, v in ffn_prm.items()}
    x = run_l0ffn(x, prm)
    x = run_l0attnb(x, np.asarray(I["L0_attn_b_Wq"], np.float32),
                    np.asarray(I["L0_attn_b_bq"], np.float32),
                    np.asarray(I["L0_attn_b_Wk"], np.float32),
                    np.asarray(I["L0_attn_b_bk"], np.float32),
                    np.asarray(I["L0_attn_b_Wv"], np.float32),
                    np.asarray(I["L0_attn_b_bv"], np.float32),
                    np.asarray(I["L0_attn_b_gamma"], np.float32),
                    np.asarray(I["L0_attn_b_beta"], np.float32))
    x = run_l0ffn(x, prm)
    dprm = {k: np.asarray(v, np.float32) for k, v in delta_prm.items()}
    x = run_delta(x, dprm)
    x = run_l1ffn(x, np.asarray(I["L1_ffn_w1"], np.float32),
                  np.asarray(I["L1_ffn_b1"], np.float32),
                  np.asarray(I["L1_ffn_w2"], np.float32),
                  np.asarray(I["L1_ffn_b2"], np.float32))
    rprm = {k: np.asarray(v, np.float32) for k, v in rwkv_prm.items()}
    x = run_l1attnb(x, rprm)
    x = run_l1ffn(x, np.asarray(I["L1_ffn_w1"], np.float32),
                  np.asarray(I["L1_ffn_b1"], np.float32),
                  np.asarray(I["L1_ffn_w2"], np.float32),
                  np.asarray(I["L1_ffn_b2"], np.float32))
    return x
